# version 21
# baseline (speedup 1.0000x reference)
import numpy as np
import ml_dtypes
import concourse.bass as bass
import concourse.mybir as mybir
from concourse.bass_utils import run_bass_kernel_spmd

F32 = mybir.dt.float32
BF16 = mybir.dt.bfloat16
F16 = mybir.dt.float16
I32 = mybir.dt.int32
AF = mybir.ActivationFunctionType
ALU = mybir.AluOpType
AX = mybir.AxisListType
bfd = ml_dtypes.bfloat16

D = 4096
S = 2048
BATCH = 4
H = 32
NBLK = 16
NSLOT = 8
RQ = 1024
EPS = 1e-5
QSCALE = 128.0 ** -0.5
NEG = -1e30

EJ = [2 * (j + 1) for j in range(NSLOT)]
SJ = [(e + 3) // 4 for e in EJ]
KCOLS = [s * 512 for s in SJ]

QL = {0: [0, 3, 4, 7, 8, 11, 12, 15], 1: [1, 2, 5, 6, 9, 10, 13, 14]}

ENGS = ("sync", "gpsimd", "vector", "scalar", "tensor")
DMA_SEMS = ("QC", "X0", "X1", "XR", "A0", "A1", "A2", "A3", "A4", "A5",
            "W0", "W1",
            "WC", "QW0", "QW1", "QW2", "QW3", "S0", "S1", "TP")
SEM_AMT = {**{s: 16 for s in DMA_SEMS}, "PE": 1, "DV": 1, "AC": 1, "PO": 1}


def build_program(debug=False, race_check=True):
    nc = bass.Bass("TRN2", target_bir_lowering=False,
                   detect_race_conditions=race_check)

    xb = nc.dram_tensor("xb", [S, D], F32, kind="ExternalInput").ap()
    wq = nc.dram_tensor("wq", [D, D], F32, kind="ExternalInput").ap()
    wkv = nc.dram_tensor("wkv", [D, 256], F32, kind="ExternalInput").ap()
    wo = nc.dram_tensor("wo", [D, D], F32, kind="ExternalInput").ap()
    qw = nc.dram_tensor("qw", [D, 3072], I32, kind="ExternalInput").ap()
    fsc_d = nc.dram_tensor("fsc", [128, 32], F32, kind="ExternalInput").ap()
    fzp_d = nc.dram_tensor("fzp", [128, 32], F32, kind="ExternalInput").ap()
    fbias_d = nc.dram_tensor("fbias", [3, 1536], F32,
                             kind="ExternalInput").ap()
    atab = nc.dram_tensor("atab", [H * NSLOT, 2048], F16,
                          kind="ExternalInput").ap()
    tri_d = nc.dram_tensor("tri", [128, 128], F32, kind="ExternalInput").ap()
    idf_d = nc.dram_tensor("idf", [128, 128], F32, kind="ExternalInput").ap()
    ones_d = nc.dram_tensor("ones1", [65, 128], F16,
                            kind="ExternalInput").ap()
    out_d = nc.dram_tensor("out", [RQ, D], F32, kind="ExternalOutput").ap()

    q = {e: [] for e in ENGS}
    cnt = {s: 0 for s in SEM_AMT}
    last_wait = {e: {s: 0 for s in SEM_AMT} for e in ENGS}
    pending = {}

    def E(eng, fn, waits=(), inc=None):
        w = list(pending.pop(eng, {}).items()) + list(waits)
        lw = last_wait[eng]
        eff = []
        for s, v in w:
            if v is not None and v > lw[s]:
                lw[s] = v
                eff.append((s, v))
        m = None
        if inc is not None:
            cnt[inc] += SEM_AMT[inc]
            m = cnt[inc]
        q[eng].append((tuple(eff), fn, inc))
        return m

    def bar():
        snap = dict(cnt)
        for e in ENGS:
            p = pending.setdefault(e, {})
            for s, v in snap.items():
                if v > p.get(s, 0):
                    p[s] = v

    dbg_tensors = {}

    def tap(name, sb_t, shape, dt, waits=()):
        if not debug:
            return None
        d = nc.dram_tensor("dbg_" + name, shape, F32,
                           kind="ExternalOutput").ap()
        return E("gpsimd", (lambda d, sb_t: lambda e: e.dma_start(
            d, sb_t[:]))(d, sb_t), waits=waits, inc="TP")

    guards = {}

    uid = [0]

    def sopen(name, shape, dt, side="left"):
        uid[0] += 1
        g = nc.sbuf_tensor(f"{name}_u{uid[0]}", shape, dt, side=side)
        guards[name] = g
        return g.__enter__()

    def sclose(*names):
        for n in names:
            guards.pop(n).__exit__(None, None, None)

    idf = sopen("idf_sb", [128, 128], F32)
    idb = sopen("idb_sb", [128, 128], F16)
    ones1 = sopen("ones_sb", [65, 128], F16)
    fsc = sopen("fsc_sb", [128, 32], F32)
    fzp = sopen("fzp_sb", [128, 32], F32)
    nzs = fzp
    fbias = sopen("fbias_bf", [65, 1536], F16)
    epsb = sopen("epsb", [128, 1], F32)
    kT = sopen("kT", [128, 2048], F16)
    vT = sopen("vT", [128, NBLK * 128], F16)
    tri = sopen("tri_sb", [128, 128], F32)

    psA_g = nc.psum_tensor("psA", [128, 2048], F32)
    psA = psA_g.__enter__()
    psB_g = nc.psum_tensor("psB", [128, 2048], F32)
    psB = psB_g.__enter__()

    sems = {}
    sem_guards = []
    for s in SEM_AMT:
        g = nc.semaphore("sem" + s)
        sems[s] = g.__enter__()
        sem_guards.append(g)

    E("sync", lambda e: e.dma_start(tri[:], tri_d), inc="QC")
    E("sync", lambda e: e.dma_start(idf[:], idf_d), inc="QC")
    E("sync", lambda e: e.dma_start(ones1[:], ones_d), inc="QC")
    E("sync", lambda e: e.dma_start(fsc[:], fsc_d), inc="QC")
    m_cl = E("sync", lambda e: e.dma_start(fzp[:], fzp_d), inc="QC")
    E("vector", lambda e: e.tensor_copy(idb[:], idf[:]), waits=[("QC", m_cl)])
    E("vector", lambda e: e.memset(epsb[:], EPS))
    m_consts = E("vector", lambda e: e.scalar_tensor_tensor(
        fzp[:], fzp[:], -1.0, fsc[:], ALU.mult, ALU.mult), inc="DV")

    ln1T = sopen("ln1T", [128, 32 * 1024], F16, side="right")
    wkv_bf = sopen("wkv_bf", [128, 32 * 256], F16, side="right")
    xrt = [sopen(f"xrt{i}", [128, D], F32) for i in range(3)]
    lnbf = [sopen(f"lnbf{i}", [128, D], F16) for i in range(2)]
    scr_s = sopen("scr_s", [128, D], F32)
    scr_q = sopen("scr_q", [128, D], F32)
    roll = sopen("roll", [128, 32 * 128], F16)
    kvbf = [sopen(f"kvbf{i}", [128, 256], F16) for i in range(2)]
    st = {n: sopen("st_" + n, [128, NBLK], F32)
          for n in ("sum", "ssq", "mean", "msq", "var", "lnv", "rstd", "nmr")}

    for r3 in range(3):
        m_fb = E("gpsimd", (lambda r3: lambda e: e.dma_start(
            fbias[32 * r3:32 * r3 + 1, :], fbias_d[r3:r3 + 1, :]))(r3),
            inc="WC")
    m_wkv = E("gpsimd", lambda e: e.dma_start(
        wkv_bf[:].rearrange("p (s c) -> p s c", s=32),
        wkv.rearrange("(s p) c -> p s c", p=128)), inc="WC")

    psKV = [psA[:, 0:256], psA[:, 512:768]]
    psT = [psB[:, 0:512], psB[:, 512:1024]]
    psKT = [psB[:, 1024:1152], psB[:, 1536:1664]]

    m_x = {}
    m_apply = {}
    prev_m1 = [None, None]
    m_tev = {}
    m_kv = {}
    m_kvev = {}
    m_ktev = {}
    tb = 0

    XSEM = ["X0", "X1", "XR"]

    def xdma(rt, wl):
        m_x[rt] = E("sync" if rt % 2 == 0 else "scalar",
                    (lambda r: lambda e: e.dma_start(
            xrt[r % 3][:], xb[r * 128:(r + 1) * 128, :]))(rt),
            waits=wl, inc=XSEM[rt % 3])

    xdma(0, [])
    xdma(1, [])
    xdma(2, [])

    def emit_stats(rt):
        nonlocal_ = prev_m1
        r2 = rt % 2
        wl0 = [(XSEM[rt % 3], m_x[rt])]
        if prev_m1[0] is not None:
            wl0.append(("DV", prev_m1[1]))
        r3 = rt % 3
        m1 = E("scalar", (lambda r, r3: lambda e: e.activation(
            scr_s[:], xrt[r3][:], AF.Identity, bias=0.0, scale=1.0,
            accum_out=st["sum"][:, r:r + 1]))(rt, r3),
            waits=wl0, inc="AC")
        m2 = E("vector", (lambda r, r3: lambda e: e.scalar_tensor_tensor(
            scr_q[:], xrt[r3][:], 1.0, xrt[r3][:], ALU.mult, ALU.mult,
            accum_out=st["ssq"][:, r:r + 1]))(rt, r3), waits=wl0, inc="DV")
        prev_m1[0], prev_m1[1] = m1, m2
        m3 = E("vector", (lambda r: lambda e: e.tensor_scalar(
            st["mean"][:, r:r + 1], st["sum"][:, r:r + 1], 1.0 / D, None,
            ALU.mult))(rt), waits=[("AC", m1)], inc="DV")
        m4 = E("vector", (lambda r: lambda e: e.tensor_tensor(
            st["msq"][:, r:r + 1], st["mean"][:, r:r + 1],
            st["mean"][:, r:r + 1], ALU.mult))(rt), waits=[("DV", m3)],
            inc="DV")
        m_var = E("vector", (lambda r: lambda e: e.scalar_tensor_tensor(
            st["var"][:, r:r + 1], st["ssq"][:, r:r + 1], 1.0 / D,
            st["msq"][:, r:r + 1], ALU.mult, ALU.subtract))(rt),
            waits=[("DV", m2), ("DV", m4)], inc="DV")
        m_lnv = E("scalar", (lambda r: lambda e: e.activation(
            st["lnv"][:, r:r + 1], st["var"][:, r:r + 1], AF.Ln, bias=epsb[:],
            scale=1.0))(rt), waits=[("DV", m_var)], inc="AC")
        m_rstd = E("scalar", (lambda r: lambda e: e.activation(
            st["rstd"][:, r:r + 1], st["lnv"][:, r:r + 1], AF.Exp, bias=0.0,
            scale=-0.5))(rt), waits=[("AC", m_lnv)], inc="AC")
        m_nmr = E("vector", (lambda r: lambda e: e.scalar_tensor_tensor(
            st["nmr"][:, r:r + 1], st["mean"][:, r:r + 1], -1.0,
            st["rstd"][:, r:r + 1], ALU.mult, ALU.mult))(rt),
            waits=[("AC", m_rstd)], inc="DV")
        wl = [("DV", m_nmr)]
        if rt - 2 in m_kv:
            wl.append(("PE", m_kv[rt - 2]))
        m_apply[rt] = E("scalar", (lambda r, r2, r3: lambda e:
                                   e.activation(
            lnbf[r2][:], xrt[r3][:], AF.Identity, bias=st["nmr"][:, r:r + 1],
            scale=st["rstd"][:, r:r + 1]))(rt, r2, r3), waits=wl, inc="AC")

        if rt + 3 < NBLK:
            xdma(rt + 3, [("AC", m_apply[rt])])

    emit_stats(0)
    for rt in range(NBLK):
        r2 = rt % 2
        if rt + 1 < NBLK:
            emit_stats(rt + 1)
        odd = (rt % 2 == 1)
        j = rt // 2
        rt_tb0 = tb
        for bt in range(8):
            wl = [("AC", m_apply[rt])]
            if tb - 2 in m_tev:
                wl.append(("DV", m_tev[tb - 2]))
            m = None
            for i in range(4):
                kc = bt * 4 + i
                m = E("tensor", (lambda r2, kc, t2, i: lambda e: e.matmul(
                    psT[t2][:, i * 128:(i + 1) * 128],
                    lnbf[r2][:, kc * 128:(kc + 1) * 128], idb[:],
                    start=True, stop=True))(r2, kc, tb % 2, i),
                    waits=wl if i == 0 else (), inc="PE" if i == 3 else None)
            if odd:
                def mk_ev(jj, bt, t2):
                    def f(e):
                        d = ln1T[:].rearrange("p (kc r) -> p kc r", r=1024)
                        d = d[:, bt * 4:(bt + 1) * 4,
                              jj * 128:jj * 128 + 128]
                        return e.tensor_copy(d, psT[t2][:].rearrange(
                            "p (k c) -> p k c", k=4))
                    return f
                m_tev[tb] = E("vector", mk_ev(j, bt, tb % 2),
                              waits=[("PE", m)], inc="DV")
            else:
                m_tev[tb] = E("vector", (lambda bt, t2: lambda e:
                                         e.tensor_copy(
                    roll[:, bt * 512:(bt + 1) * 512], psT[t2][:]))(bt, tb % 2),
                    waits=[("PE", m)], inc="DV")
            tb += 1

        if odd:
            src = (lambda jj: lambda kc: ln1T[
                :, kc * 1024 + jj * 128: kc * 1024 + jj * 128 + 128])(j)
        else:
            src = lambda kc: roll[:, kc * 128:(kc + 1) * 128]
        mkv = None
        for kc in range(32):
            wl = []
            if kc % 4 == 0:
                wl.append(("DV", m_tev[rt_tb0 + kc // 4]))
            if kc == 0:
                wl.append(("WC", m_wkv))
                if rt - 2 in m_kvev:
                    wl.append(("DV", m_kvev[rt - 2]))
            mkv = E("tensor", (lambda kc, r2, src: lambda e: e.matmul(
                psKV[r2], src(kc), wkv_bf[:, kc * 256:(kc + 1) * 256],
                start=(kc == 0), stop=(kc == 31)))(kc, r2, src),
                waits=wl, inc="PE" if kc == 31 else None)
        m_kv[rt] = mkv
        m_kvev[rt] = E("vector", (lambda r2: lambda e: e.tensor_copy(
            kvbf[r2][:], psKV[r2]))(r2), waits=[("PE", m_kv[rt])], inc="DV")
        E("vector", (lambda r, r2: lambda e: e.tensor_copy(
            vT[:, r * 128:(r + 1) * 128], kvbf[r2][:, 128:256]))(rt, r2),
            waits=[("DV", m_kvev[rt])])
        wl = [("DV", m_kvev[rt])]
        if rt - 2 in m_ktev:
            wl.append(("DV", m_ktev[rt - 2]))
        m_kt = E("tensor", (lambda r2: lambda e: e.matmul(
            psKT[r2], kvbf[r2][:, 0:128], idb[:], start=True,
            stop=True))(r2), waits=wl, inc="PE")
        m_ktev[rt] = E("vector", (lambda r, r2: lambda e: e.tensor_copy(
            kT[:, r * 128:(r + 1) * 128], psKT[r2]))(rt, r2),
            waits=[("PE", m_kt)], inc="DV")

    bar()
    tap("kT", kT, [128, 2048], F16)
    tap("vT", vT, [128, NBLK * 128], F16)
    tap("ln1T", ln1T, [128, 32 * 1024], F16)
    bar()
    sclose(*("st_" + n for n in reversed(("sum", "ssq", "mean", "msq", "var",
                                          "lnv", "rstd", "nmr"))))
    sclose("kvbf1", "kvbf0", "roll", "scr_q", "scr_s", "lnbf1", "lnbf0",
           "xrt2", "xrt1", "xrt0")
    sclose("wkv_bf")

    qT = sopen("qT", [128, 32 * 1024], F16)
    wqb = [sopen(f"wqb{i}", [128, 8 * 512], F16) for i in range(2)]


    psWq = [psA[:, t * 512:(t + 1) * 512] for t in range(4)] + \
           [psB[:, t * 512:(t + 1) * 512] for t in range(4)]

    m_wq_done = {}
    m_wqc = {}
    m_wqev = {}

    def wq_dma(ci, wl):
        ntg, kcs = ci // 4, ci % 4
        m_wqc[ci] = E("gpsimd", (lambda ntg, kcs, ci: lambda e: e.dma_start(
            wqb[ci % 2][:].rearrange("p (s c) -> p s c", s=8),
            wq[kcs * 1024:(kcs + 1) * 1024,
               ntg * 512:(ntg + 1) * 512].rearrange(
                "(s p) c -> p s c", p=128)))(ntg, kcs, ci),
            waits=wl, inc="W%d" % (ci % 2))

    wq_dma(0, [])
    wq_dma(1, [])
    for ntg in range(8):
        for kcs in range(4):
            ci = ntg * 4 + kcs
            mlast = None
            for kc8 in range(8):
                kc = kcs * 8 + kc8
                for i in range(4):
                    for rs in range(2):
                        wl = []
                        if kc8 == 0 and i == 0 and rs == 0:
                            wl.append(("W%d" % (ci % 2), m_wqc[ci]))
                            if kc == 0 and ntg - 1 in m_wqev:
                                wl.append(("DV", m_wqev[ntg - 1]))
                        mlast = E("tensor", (lambda ci, kc8, i, rs, kc:
                                             lambda e: e.matmul(
                            psWq[i * 2 + rs],
                            wqb[ci % 2][:, kc8 * 512 + i * 128:
                                        kc8 * 512 + i * 128 + 128],
                            ln1T[:, kc * 1024 + rs * 512:
                                 kc * 1024 + rs * 512 + 512],
                            start=(kc == 0),
                            stop=(kc == 31)))(ci, kc8, i, rs, kc),
                            waits=wl,
                            inc="PE" if (kc8 == 7 and i == 3 and rs == 1)
                            else None)
            m_wq_done[ci] = mlast
            if ci + 2 < 32:
                wq_dma(ci + 2, [("PE", m_wq_done[ci])])
        for i in range(4):
            for rs in range(2):
                h = ntg * 4 + i
                mm = E("vector", (lambda h, rs, i: lambda e: e.tensor_scalar(
                    qT[:, h * 1024 + rs * 512: h * 1024 + rs * 512 + 512],
                    psWq[i * 2 + rs], QSCALE, None, ALU.mult))(h, rs, i),
                    waits=[("PE", m_wq_done[ntg * 4 + 3])]
                    if (i == 0 and rs == 0) else (),
                    inc="DV" if (i == 3 and rs == 1) else None)
        m_wqev[ntg] = mm

    bar()
    tap("qT", qT, [128, 32 * 1024], F16)
    bar()
    sclose("wqb1", "wqb0")
    sclose("ln1T")

    AVT_B = sopen("AVT_B", [128, 32 * 512], F16, side="right")
    AVT_A = sopen("AVT_A", [128, 32 * 512], F16, side="right")
    attn_sb = [sopen(f"attn_sb{i}", [128, 4 * 2048], F16) for i in range(2)]
    diagm = [sopen(f"diagm{i}", [128, 4 * 128], F16) for i in range(2)]
    atst = [sopen(f"atst{i}", [1, 2048], F16) for i in range(6)]
    attnT = [sopen(f"attnT{i}", [128, 512], F16) for i in range(2)]
    lbuf = sopen("lbuf", [128, 4], F32)
    linv = sopen("linv", [128, 4], F32)
    negmax = sopen("negmax", [128, 2], F32)
    pmax = sopen("pmax", [128, 16], F32)

    psT2 = [psB[:, 0:512], psB[:, 512:1024]]
    psAVT = [psB[:, 1024:1536], psB[:, 1536:2048]]

    m_ali = {}
    m_rank1 = {}
    m_exp = {}
    m_Btr = {}
    m_Tev2 = {}
    m_AVTev = {}
    m_diag = {}
    bstate = {}
    a_ctr = 0

    def prev2B(j, g):
        if g >= 2:
            return (j, g - 2)
        if j >= 1:
            return (j - 1, g + 6)
        return None

    def emit_B_chunk(bj, bg, kbs):
        e_b = EJ[bj]
        st = bstate.setdefault((bj, bg), {})

        def emit_av(kb):
            wl = [("DV", m_Tev2[(bj, bg, kb)])]
            if kb == 0:
                pk = prev2B(bj, bg)
                if pk in m_AVTev:
                    wl.append(("DV", m_AVTev[pk]))
            st["m_av"] = E("tensor", (lambda kb, bg, e_b: lambda e: e.matmul(
                psAVT[bg % 2], vT[:, kb * 128:(kb + 1) * 128],
                attnT[kb % 2][:],
                start=(kb == 0), stop=(kb == e_b - 1)))(kb, bg, e_b),
                waits=wl, inc="PE" if kb == e_b - 1 else None)

        for kb in kbs:
            wl = []
            if (bj, bg, kb - 2) in m_Tev2:
                wl.append(("DV", m_Tev2[(bj, bg, kb - 2)]))
            if kb == 0:
                wl.append(("AC", m_diag[(bj, bg)]))
            mT = None
            for hh in range(4):
                mT = E("tensor", (lambda kb, hh, bg: lambda e: e.matmul(
                    psT2[kb % 2][:, hh * 128:(hh + 1) * 128],
                    attn_sb[bg % 2][:, hh * 2048 + kb * 128:
                                    hh * 2048 + kb * 128 + 128],
                    diagm[bg % 2][:, hh * 128:(hh + 1) * 128],
                    start=True, stop=True))(kb, hh, bg),
                    waits=wl if hh == 0 else (),
                    inc="PE" if hh == 3 else None)
            m_Tev2[(bj, bg, kb)] = E("vector", (lambda kb: lambda e:
                                               e.tensor_copy(
                attnT[kb % 2][:], psT2[kb % 2][:]))(kb),
                waits=[("PE", mT)], inc="DV")
            if "pend" in st:
                emit_av(st.pop("pend"))
            st["pend"] = kb
        if kbs and kbs[-1] == e_b - 1:
            emit_av(st.pop("pend"))
            m_Btr[(bj, bg)] = st["m_av"]
            AVT = AVT_A if bj < 4 else AVT_B
            jw = bj % 4

            def mk_avtev(AVT, bg, jw):
                def f(e):
                    d = AVT[:].rearrange("p (h r) -> p h r", r=512)
                    d = d[:, bg * 4:(bg + 1) * 4, jw * 128: jw * 128 + 128]
                    return e.tensor_copy(d, psAVT[bg % 2].rearrange(
                        "p (h c) -> p h c", h=4))
                return f
            m_AVTev[(bj, bg)] = E("vector", mk_avtev(AVT, bg, jw),
                                  waits=[("PE", st["m_av"])], inc="DV")

    prevB = None
    for j in range(NSLOT):
        K = KCOLS[j]
        s = SJ[j]
        e_j = EJ[j]
        depth2 = K <= 1024
        for g in range(8):
            if prevB is not None:
                pe_b = EJ[prevB[0]]
                kbs_all = list(range(pe_b))
                chunks = [kbs_all[(c * pe_b) // 4:((c + 1) * pe_b) // 4]
                          for c in range(4)]
            else:
                chunks = [[], [], [], []]
            for hh in range(4):
                h = g * 4 + hh
                a = a_ctr
                a_ctr += 1
                wl = []
                if a - 6 >= 0:
                    wl.append(("PE", m_rank1[a - 6]))
                m_ali[a] = E("sync", (lambda j, h, a, K: lambda e:
                                      e.dma_start(
                    atst[a % 6][0:1, 0:K],
                    atab[h * NSLOT + j: h * NSLOT + j + 1, 0:K]))(
                        j, h, a, K),
                    waits=wl, inc="A%d" % (a % 6))

                if depth2:
                    roff = (hh % 2) * 1024
                    prev = (j, g, hh - 2) if hh >= 2 else                         ((j, g - 1, hh + 2) if g >= 1 else
                         ((j - 1, 7, hh + 2) if j >= 1 else None))
                else:
                    roff = 0
                    prev = (j, g, hh - 1) if hh >= 1 else                         ((j, g - 1, 3) if g >= 1 else
                         ((j - 1, 7, 3) if j >= 1 else None))
                wl = [("A%d" % (a % 6), m_ali[a])]
                if prev in m_exp:
                    wl.append(("AC", m_exp[prev]))
                mr = None
                for i in range(s):
                    mr = E("tensor", (lambda roff, i, a: lambda e: e.matmul(
                        psA[:, roff + i * 512: roff + (i + 1) * 512],
                        ones1[0:1, :],
                        atst[a % 6][0:1, i * 512:(i + 1) * 512],
                        start=True, stop=False))(roff, i, a),
                        waits=wl if i == 0 else (),
                        inc="PE" if i == s - 1 else None)
                m_rank1[a] = mr
                mqk = []
                for i in range(s):
                    mqk.append(E("tensor", (lambda roff, i, h, j: lambda e:
                                            e.matmul(
                        psA[:, roff + i * 512: roff + (i + 1) * 512],
                        qT[:, h * 1024 + j * 128: h * 1024 + j * 128 + 128],
                        kT[:, i * 512:(i + 1) * 512],
                        start=False, stop=True))(roff, i, h, j), inc="PE"))
                dslab = (2 * j + 1) // 4
                m_tri = E("vector", (lambda roff, j: lambda e:
                                     e.tensor_tensor(
                    psA[:, roff + (2 * j + 1) * 128:
                        roff + (2 * j + 2) * 128],
                    psA[:, roff + (2 * j + 1) * 128:
                        roff + (2 * j + 2) * 128], tri[:],
                    ALU.add))(roff, j),
                    waits=[("PE", mqk[dslab])], inc="DV")
                m_pm = None
                for i in range(s):
                    wlp = [("PE", mqk[i])]
                    if i == dslab:
                        wlp.append(("DV", m_tri))
                    m_pm = E("vector", (lambda roff, i, hh: lambda e:
                                        e.reduce_max(
                        pmax[:, hh * 4 + i:hh * 4 + i + 1],
                        psA[:, roff + i * 512: roff + (i + 1) * 512],
                        AX.X, negate=True))(roff, i, hh),
                        waits=wlp, inc="DV")
                m_cmb = E("vector", (lambda hh, s: lambda e: e.tensor_reduce(
                    negmax[:, hh % 2:hh % 2 + 1],
                    pmax[:, hh * 4:hh * 4 + s], AX.X, ALU.min))(hh, s),
                    waits=[("DV", m_pm)], inc="DV")
                wl2 = [("DV", m_cmb)]
                pb = prev2B(j, g)
                if pb in m_Btr:
                    wl2.append(("PE", m_Btr[pb]))
                m_exp[(j, g, hh)] = E("scalar", (lambda roff, K, hh, g:
                                                 lambda e: e.activation(
                    attn_sb[g % 2][:, hh * 2048: hh * 2048 + K],
                    psA[:, roff:roff + K],
                    AF.Exp, bias=negmax[:, hh % 2:hh % 2 + 1], scale=1.0,
                    accum_out=lbuf[:, hh:hh + 1]))(roff, K, hh, g),
                    waits=wl2, inc="AC")
                if prevB is not None:
                    emit_B_chunk(prevB[0], prevB[1], chunks[hh])
            wl = [("AC", m_exp[(j, g, 3)])]
            m_rcp = E("vector", lambda e: e.reciprocal(linv[:], lbuf[:]),
                      waits=wl, inc="DV")
            pk = prev2B(j, g)
            md = None
            for hh in range(4):
                wl = [("DV", m_rcp)] if hh == 0 else ()
                if hh == 0 and pk in m_Btr:
                    wl.append(("PE", m_Btr[pk]))
                md = E("scalar", (lambda hh, g: lambda e: e.activation(
                    diagm[g % 2][:, hh * 128:(hh + 1) * 128], idf[:],
                    AF.Identity, bias=0.0,
                    scale=linv[:, hh:hh + 1]))(hh, g),
                    waits=wl, inc="AC" if hh == 3 else None)
            m_diag[(j, g)] = md
            prevB = (j, g)
    emit_B_chunk(prevB[0], prevB[1], list(range(EJ[prevB[0]])))

    bar()
    tap("AVT_A", AVT_A, [128, 32 * 512], F16)
    tap("AVT_B", AVT_B, [128, 32 * 512], F16)
    bar()
    sclose("pmax", "negmax", "linv", "lbuf", "attnT1", "attnT0",
           "atst5", "atst4", "atst3", "atst2", "atst1", "atst0",
           "diagm1", "diagm0",
           "attn_sb1", "attn_sb0", "qT",
           "tri_sb", "vT", "kT")

    x2s = sopen("x2s", [128, 4 * D], F32)
    ln2Ts = [sopen("ln2T0", [128, 32 * 512], F16)]
    x2sp = nc.dram_tensor("x2sp", [512, D], F32, kind="Internal").ap()
    m_l2dones = []
    m_spill = None

    psWo = [psA[:, t * 512:(t + 1) * 512] for t in range(4)] + \
           [psB[:, t * 512:(t + 1) * 512] for t in range(4)]

    for slab in range(2):
        AVT = AVT_A if slab == 0 else AVT_B
        if slab == 1:
            ln2Ts.append(sopen("ln2T1", [128, 32 * 512], F16))
        xres = [sopen("xres0", [128, D], F32)]
        xres = [xres[0], xres[0]]
        wob = [sopen(f"wob{i}", [128, 8 * 512], F16) for i in range(2)]

        m_woc = {}
        m_wo_done = {}
        m_woev = {}

        def wo_dma(ci, wl):
            nt, kcs = ci // 4, ci % 4
            m_woc[ci] = E("gpsimd", (lambda nt, kcs, ci, wob=wob: lambda e:
                                     e.dma_start(
                wob[ci % 2][:].rearrange("p (s c) -> p s c", s=8),
                wo[kcs * 1024:(kcs + 1) * 1024,
                   nt * 512:(nt + 1) * 512].rearrange(
                    "(s p) c -> p s c", p=128)))(nt, kcs, ci),
                waits=wl, inc="W%d" % (ci % 2))

        wo_dma(0, [])
        wo_dma(1, [])
        for nt in range(8):
            for kcs in range(4):
                ci = nt * 4 + kcs
                mlast = None
                for kc8 in range(8):
                    kc = kcs * 8 + kc8
                    for rtw in range(4):
                        wl = []
                        if kc8 == 0 and rtw == 0:
                            wl.append(("W%d" % (ci % 2), m_woc[ci]))
                            if kc == 0 and nt - 2 in m_woev:
                                wl.append(("DV", m_woev[nt - 2]))
                        mlast = E("tensor", (lambda ci, kc8, kc, rtw, nt,
                                             AVT=AVT, wob=wob:
                                             lambda e: e.matmul(
                            psWo[(nt % 2) * 4 + rtw],
                            AVT[:, kc * 512 + rtw * 128:
                                kc * 512 + rtw * 128 + 128],
                            wob[ci % 2][:, kc8 * 512:(kc8 + 1) * 512],
                            start=(kc == 0),
                            stop=(kc == 31)))(ci, kc8, kc, rtw, nt),
                            waits=wl,
                            inc="PE" if (kc8 == 7 and rtw == 3) else None)
                m_wo_done[ci] = mlast
                if ci + 2 < 32:
                    wo_dma(ci + 2, [("PE", m_wo_done[ci])])
            mm = None
            for rtw in range(4):
                wl = []
                if rtw == 0:
                    wl = [("PE", m_wo_done[nt * 4 + 3])]
                    if slab == 1 and nt == 0:
                        wl.append(("S1", m_spill))
                mm = E("vector", (lambda nt, rtw: lambda e: e.tensor_copy(
                    x2s[:, rtw * D + nt * 512: rtw * D + nt * 512 + 512],
                    psWo[(nt % 2) * 4 + rtw]))(nt, rtw),
                    waits=wl,
                    inc="DV" if rtw == 3 else None)
            m_woev[nt] = mm

        m_xr = {}
        m_res = {}

        def xr_dma(rtw, wl):
            j = slab * 4 + rtw
            m_xr[rtw] = E("sync", (lambda j, rtw, xres=xres: lambda e:
                e.dma_start(
                xres[rtw % 2][:],
                xb[(2 * j + 1) * 128:(2 * j + 2) * 128, :]))(j, rtw),
                waits=wl, inc="XR")

        sclose("wob1", "wob0")
        sclose("AVT_A" if slab == 0 else "AVT_B")
        xr_dma(0, [])

        lnbf2 = [sopen(f"lnbf2_{i}", [128, D], F16) for i in range(2)]
        scr2s = sopen("scr2s", [128, D], F32)
        scr2q = sopen("scr2q", [128, D], F32)
        s2 = {n: sopen("s2_" + n, [128, 4], F32)
              for n in ("sum", "ssq", "mean", "msq", "var", "lnv", "rstd",
                        "nmr")}

        m_l2ap = {}
        m_l2tev = {}
        m_l2kv = {}
        tb3 = 0
        for rtw in range(4):
            m_res[rtw] = E("vector", (lambda rtw, xres=xres: lambda e:
                e.tensor_tensor(
                x2s[:, rtw * D:(rtw + 1) * D], x2s[:, rtw * D:(rtw + 1) * D],
                xres[rtw % 2][:], ALU.add))(rtw),
                waits=[("XR", m_xr[rtw]), ("DV", m_woev[7])], inc="DV")
            if rtw + 1 < 4:
                xr_dma(rtw + 1, [("DV", m_res[rtw])])
            wl0 = [("DV", m_res[rtw])]
            m1 = E("scalar", (lambda rtw, scr2s=scr2s, s2=s2: lambda e:
                e.activation(
                scr2s[:], x2s[:, rtw * D:(rtw + 1) * D], AF.Identity,
                bias=0.0, scale=1.0,
                accum_out=s2["sum"][:, rtw:rtw + 1]))(rtw),
                waits=wl0, inc="AC")
            m2 = E("vector", (lambda rtw, scr2q=scr2q, s2=s2: lambda e:
                e.scalar_tensor_tensor(
                scr2q[:], x2s[:, rtw * D:(rtw + 1) * D], 1.0,
                x2s[:, rtw * D:(rtw + 1) * D], ALU.mult, ALU.mult,
                accum_out=s2["ssq"][:, rtw:rtw + 1]))(rtw), inc="DV")
            m3 = E("vector", (lambda rtw, s2=s2: lambda e: e.tensor_scalar(
                s2["mean"][:, rtw:rtw + 1], s2["sum"][:, rtw:rtw + 1],
                1.0 / D, None, ALU.mult))(rtw), waits=[("AC", m1)],
                inc="DV")
            m4 = E("vector", (lambda rtw, s2=s2: lambda e: e.tensor_tensor(
                s2["msq"][:, rtw:rtw + 1], s2["mean"][:, rtw:rtw + 1],
                s2["mean"][:, rtw:rtw + 1], ALU.mult))(rtw),
                waits=[("DV", m3)], inc="DV")
            m_var = E("vector", (lambda rtw, s2=s2: lambda e:
                e.scalar_tensor_tensor(
                s2["var"][:, rtw:rtw + 1], s2["ssq"][:, rtw:rtw + 1],
                1.0 / D, s2["msq"][:, rtw:rtw + 1], ALU.mult,
                ALU.subtract))(rtw), waits=[("DV", m2), ("DV", m4)],
                inc="DV")
            m_lnv = E("scalar", (lambda rtw, s2=s2: lambda e: e.activation(
                s2["lnv"][:, rtw:rtw + 1], s2["var"][:, rtw:rtw + 1], AF.Ln,
                bias=epsb[:], scale=1.0))(rtw), waits=[("DV", m_var)],
                inc="AC")
            m_rstd = E("scalar", (lambda rtw, s2=s2: lambda e: e.activation(
                s2["rstd"][:, rtw:rtw + 1], s2["lnv"][:, rtw:rtw + 1],
                AF.Exp, bias=0.0, scale=-0.5))(rtw), waits=[("AC", m_lnv)],
                inc="AC")
            m_nmr = E("vector", (lambda rtw, s2=s2: lambda e:
                                 e.scalar_tensor_tensor(
                s2["nmr"][:, rtw:rtw + 1], s2["mean"][:, rtw:rtw + 1], -1.0,
                s2["rstd"][:, rtw:rtw + 1], ALU.mult, ALU.mult))(rtw),
                waits=[("AC", m_rstd)], inc="DV")
            wl = [("DV", m_nmr)]
            if rtw - 2 in m_l2kv:
                wl.append(("PE", m_l2kv[rtw - 2]))
            m_l2ap[rtw] = E("scalar", (lambda rtw, lnbf2=lnbf2, s2=s2:
                lambda e: e.activation(
                lnbf2[rtw % 2][:], x2s[:, rtw * D:(rtw + 1) * D], AF.Identity,
                bias=s2["nmr"][:, rtw:rtw + 1],
                scale=s2["rstd"][:, rtw:rtw + 1]))(rtw), waits=wl, inc="AC")
            if slab == 0:
                m_spill = E("sync", (lambda rtw: lambda e: e.dma_start(
                    x2sp[rtw * 128:(rtw + 1) * 128, :],
                    x2s[:, rtw * D:(rtw + 1) * D]))(rtw),
                    waits=[("AC", m_l2ap[rtw])], inc="S1")
            mT = None
            for bt in range(8):
                wl = [("AC", m_l2ap[rtw])]
                if tb3 - 2 in m_l2tev:
                    wl.append(("DV", m_l2tev[tb3 - 2]))
                for i in range(4):
                    kc = bt * 4 + i
                    mT = E("tensor", (lambda rtw, kc, t3, i, lnbf2=lnbf2:
                                      lambda e:
                                      e.matmul(
                        psT[t3][:, i * 128:(i + 1) * 128],
                        lnbf2[rtw % 2][:, kc * 128:(kc + 1) * 128], idb[:],
                        start=True, stop=True))(rtw, kc, tb3 % 2, i),
                        waits=wl if i == 0 else (),
                        inc="PE" if i == 3 else None)

                def mk_l2ev(rtw, bt, t3, slab=slab):
                    def f(e):
                        d = ln2Ts[slab][:].rearrange(
                            "p (kc r) -> p kc r", r=512)
                        d = d[:, bt * 4:(bt + 1) * 4,
                              rtw * 128: rtw * 128 + 128]
                        return e.tensor_copy(d, psT[t3][:].rearrange(
                            "p (k c) -> p k c", k=4))
                    return f
                m_l2tev[tb3] = E("vector", mk_l2ev(rtw, bt, tb3 % 2),
                                 waits=[("PE", mT)], inc="DV")
                tb3 += 1
            m_l2kv[rtw] = mT
        m_l2dones.append(m_l2tev[tb3 - 1])
        if slab == 0:
            bar()
            tap("x2s", x2s, [128, 4 * D], F32)
            tap("ln2T", ln2Ts[0], [128, 32 * 512], F16)
            bar()

        sclose(*("s2_" + n for n in reversed(("sum", "ssq", "mean", "msq",
                                              "var", "lnv", "rstd", "nmr"))))
        sclose("scr2q", "scr2s", "lnbf2_1", "lnbf2_0", "xres0")

    qwb = [sopen(f"qwb{i}", [128, 768], I32) for i in range(3)]
    t0b = [sopen(f"t0b{i}", [128, 768], I32) for i in range(2)]
    t1b = [t0b[0][:, 512:768], t0b[1][:, 512:768]]
    ubf = [sopen(f"ubf{i}", [128, 1024], F16) for i in range(2)]
    wbf = [sopen(f"wbf{i}", [128, D], F16) for i in range(2)]
    wTb = [sopen(f"wTb{i}", [128, 32 * 256], F16) for i in range(2)]
    outsb = sopen("outsb", [128, 4 * 256], F32)
    x2rel = sopen("x2rel0", [128, 4 * 256], F32)

    m_qwc = {}
    m_unp = {}
    m_wtr_nti = {}
    m_ffn_done = {}
    m_fev = {}
    m_store = {}
    m_wtev = {}
    m_x2r = {}
    wtb = 0

    def qw_dma(hc, wl):
        nti, qtr = hc // 4, hc % 4
        m_qwc[hc] = E("sync", (lambda nti, qtr, hc, qwb=qwb: lambda e:
                               e.dma_start(
            qwb[hc % 3][:],
            qw[nti * 128:(nti + 1) * 128,
               qtr * 768:(qtr + 1) * 768]))(nti, qtr, hc),
            waits=wl, inc="QW%d" % (hc % 3))

    def x2r_dma(grp, wl):
        ms = None
        for t in range(4):
            ms = E("scalar", (lambda grp, t: lambda e: e.dma_start(
                x2rel[:, t * 256:(t + 1) * 256],
                x2sp[t * 128:(t + 1) * 128,
                     grp * 256:(grp + 1) * 256]))(grp, t),
                waits=wl if t == 0 else (), inc="X0")
        m_x2r[grp] = ms

    for i in range(3):
        qw_dma(i, [("DV", m_l2dones[1])])
    x2r_dma(0, [("S1", m_spill)])
    m_deq_by_hc = {}
    m_deqq = {}

    def emit_unpack(grp):
        for sub in range(2):
            nti = grp * 2 + sub
            for qtr in range(4):
                hc = nti * 4 + qtr
                eng = "vector"
                esem = "DV"
                e2 = hc % 2
                wl = [("QW%d" % (hc % 3), m_qwc[hc])]
                if hc - 2 in m_deq_by_hc:
                    wl.append(("AC", m_deq_by_hc[hc - 2]))
                if hc - 2 in m_unp:
                    wl.append((esem, m_unp[hc - 2]))
                m_t0 = E(eng, (lambda e2, t0b=t0b, qwb=qwb, hc=hc:
                               lambda e:
                               e.tensor_scalar(
                    t0b[e2][:], qwb[hc % 3][:], 63, None,
                    ALU.bitwise_and))(e2),
                    waits=wl, inc=esem)
                E(eng, (lambda e2, ubf=ubf, t0b=t0b: lambda e:
                    e.tensor_copy(
                    ubf[e2][:].rearrange("p (g f) -> p g f", f=4)[
                        :, :, 0:1],
                    t0b[e2][:].rearrange("p (g t) -> p g t", t=3)[
                        :, :, 0:1]))(e2),
                    waits=[(esem, m_t0)], inc=esem)
                E(eng, (lambda e2, ubf=ubf, t0b=t0b: lambda e:
                    e.tensor_copy(
                    ubf[e2][:].rearrange("p (g f) -> p g f", f=4)[
                        :, :, 2:3],
                    t0b[e2][:].rearrange("p (g t) -> p g t", t=3)[
                        :, :, 1:2]))(e2),
                    inc=esem)
                m_c3 = E(eng, (lambda e2, ubf=ubf, t0b=t0b: lambda e:
                    e.tensor_copy(
                    ubf[e2][:].rearrange("p (g f) -> p g f", f=4)[
                        :, :, 3:4],
                    t0b[e2][:].rearrange("p (g t) -> p g t", t=3)[
                        :, :, 2:3]))(e2),
                    inc=esem)
                m_t1 = E(eng, (lambda e2, t1b=t1b, qwb=qwb, hc=hc:
                               lambda e:
                               e.tensor_scalar(
                    t1b[e2],
                    qwb[hc % 3][:].rearrange("p (g t) -> p g t", t=3)[
                        :, :, 0:1],
                    6, 63, ALU.logical_shift_right,
                    ALU.bitwise_and))(e2), waits=[(esem, m_c3)],
                    inc=esem)
                m_unp[hc] = E(eng, (lambda e2, ubf=ubf, t1b=t1b:
                    lambda e: e.tensor_copy(
                    ubf[e2][:].rearrange("p (g f) -> p g f", f=4)[
                        :, :, 1:2],
                    t1b[e2].rearrange("p (g o) -> p g o", o=1)))(e2),
                    waits=[(esem, m_t1)], inc=esem)
                if hc + 3 < 128:
                    qw_dma(hc + 3, [(esem, m_unp[hc])])
                wl = [(esem, m_unp[hc])]
                if qtr == 0 and (nti - 2) in m_wtr_nti:
                    wl.append(("PE", m_wtr_nti[nti - 2]))
                m_deq_by_hc[hc] = E("scalar", (lambda nti, qtr, wbf=wbf,
                                               ubf=ubf, e2=e2: lambda e:
                                               e.activation(
                    wbf[nti % 2][:, qtr * 1024:(qtr + 1) * 1024],
                    ubf[e2][:],
                    AF.Identity, bias=nzs[:, nti:nti + 1],
                    scale=fsc[:, nti:nti + 1]))(nti, qtr), waits=wl,
                    inc="AC")
                m_deqq[(grp, sub, qtr)] = m_deq_by_hc[hc]
    emit_unpack(0)
    for grp in range(16):
        gb = grp % 2
        nti0, nti1 = grp * 2, grp * 2 + 1
        for bt in range(16):
            qq = bt // 4
            wl = [("AC", m_deqq[(grp, 0, qq)]), ("AC", m_deqq[(grp, 1, qq)])]
            if wtb - 2 in m_wtev:
                wl.append(("DV", m_wtev[wtb - 2]))
            else:
                wl.append(("DV", m_l2dones[1]))
            if bt == 0 and grp - 2 in m_ffn_done:
                wl.append(("PE", m_ffn_done[grp - 2]))
            mT = None
            for i in range(4):
                kc = bt * 2 + i // 2
                nti = nti0 if i % 2 == 0 else nti1
                mT = E("tensor", (lambda kc, wtb, i, nti, wbf=wbf:
                                  lambda e:
                                  e.matmul(
                    psT[wtb % 2][:, i * 128:(i + 1) * 128],
                    wbf[nti % 2][:, kc * 128:(kc + 1) * 128], idb[:],
                    start=True, stop=True))(kc, wtb, i, nti),
                    waits=wl if i == 0 else (),
                    inc="PE" if i == 3 else None)
            m_wtr_nti[nti0] = mT
            m_wtr_nti[nti1] = mT

            def mk_wtev(gb, bt, wtb, wTb=wTb):
                def f(e):
                    return e.tensor_copy(
                        wTb[gb][:, bt * 512:(bt + 1) * 512],
                        psT[wtb % 2][:])
                return f
            m_wtev[wtb] = E("vector", mk_wtev(gb, bt, wtb),
                            waits=[("PE", mT)], inc="DV")
            wtb += 1
        if grp + 1 < 16:
            emit_unpack(grp + 1)
        psF = [psA[:, t * 256:(t + 1) * 256] for t in range(8)]
        m_bias = None
        for t in range(8):
            for kc in range(32):
                wl = []
                if t == 0 and kc == 0:
                    wl.append(("DV", m_wtev[wtb - 1]))
                    if grp - 1 in m_fev:
                        wl.append(("DV", m_fev[grp - 1]))
                    if grp == 0:
                        wl.append(("DV", m_consts))
                        wl.append(("WC", m_fb))
                E("tensor", (lambda t, kc, gb: lambda e: e.matmul(
                    psF[t],
                    ln2Ts[t // 4][:, kc * 512 + (t % 4) * 128:
                                  kc * 512 + (t % 4) * 128 + 128],
                    wTb[gb][:, kc * 256:(kc + 1) * 256],
                    start=(kc == 0), stop=False))(t, kc, gb),
                    waits=wl, inc=None)
            m_bias = E("tensor", (lambda t, grp: lambda e: e.matmul(
                psF[t],
                ones1[32 * (grp // 6):32 * (grp // 6) + 1, :],
                fbias[32 * (grp // 6):32 * (grp // 6) + 1,
                      (grp % 6) * 256:(grp % 6) * 256 + 256],
                start=False, stop=True))(t, grp),
                inc="PE" if t == 7 else None)
        m_ffn_done[grp] = m_bias

        def mk_fev(t, grp):
            if t < 4:
                src = x2rel[:, t * 256:(t + 1) * 256]
            else:
                src = x2s[:, (t - 4) * D + grp * 256:
                          (t - 4) * D + grp * 256 + 256]

            def f(e):
                return e.tensor_tensor(
                    outsb[:, (t % 4) * 256:(t % 4) * 256 + 256], psF[t],
                    src, ALU.add)
            return f

        def mk_st(t, grp):
            def f(e):
                return e.dma_start(
                    out_d[t * 128:(t + 1) * 128,
                          grp * 256:(grp + 1) * 256],
                    outsb[:, (t % 4) * 256:(t % 4) * 256 + 256])
            return f
        mm = None
        for t in range(4):
            wl = []
            if t == 0:
                wl = [("PE", m_bias), ("X0", m_x2r[grp])]
                if grp - 1 in m_store:
                    wl.append(("S0", m_store[grp - 1]))
            mm = E("vector", mk_fev(t, grp), waits=wl,
                   inc="DV" if t == 3 else None)
        m_fevA = mm
        if grp + 1 < 16:
            x2r_dma(grp + 1, [("DV", m_fevA)])
        ms = None
        for t in range(4):
            ms = E("sync", mk_st(t, grp),
                   waits=[("DV", m_fevA)] if t == 0 else (), inc="S0")
        m_storeA = ms
        mm = None
        for t in range(4, 8):
            wl = [("S0", m_storeA)] if t == 4 else []
            mm = E("vector", mk_fev(t, grp), waits=wl,
                   inc="DV" if t == 7 else None)
        m_fev[grp] = mm
        ms = None
        for t in range(4, 8):
            ms = E("sync", mk_st(t, grp),
                   waits=[("DV", m_fev[grp])] if t == 4 else (), inc="S0")
        m_store[grp] = ms

    bar()
    sclose("x2rel0", "outsb", "wTb1", "wTb0", "wbf1", "wbf0",
           "ubf1", "ubf0", "t0b1", "t0b0",
           "qwb2", "qwb1", "qwb0")

    for s in DMA_SEMS:
        tot = cnt[s]
        if tot > 0:
            E("sync", (lambda s, tot: lambda e: e.wait_ge(sems[s], tot))(
                s, tot))

    with nc.Block() as block:
        def mk_section(name):
            def section(eng):
                for waits, fn, inc in q[name]:
                    for s, v in waits:
                        eng.wait_ge(sems[s], v)
                    inst = fn(eng)
                    if inc is not None:
                        inst.then_inc(sems[inc], SEM_AMT[inc])
            return section

        block.sync(mk_section("sync"))
        block.gpsimd(mk_section("gpsimd"))
        block.vector(mk_section("vector"))
        block.scalar(mk_section("scalar"))
        block.tensor(mk_section("tensor"))

    for g in reversed(list(guards.values())):
        g.__exit__(None, None, None)
    psB_g.__exit__(None, None, None)
    psA_g.__exit__(None, None, None)
    for g in sem_guards:
        g.__exit__(None, None, None)
    return nc


def _alibi_slopes(n):
    start = 2.0 ** (-2.0 ** (-(np.log2(n) - 3)))
    return np.array([start ** (i + 1) for i in range(n)], dtype=np.float32)


def _host_tables(half):
    ql = QL[half]
    comp = QL[1 - half]
    B = []
    for j in range(NSLOT):
        B.append(comp[j])
        B.append(ql[j])
    slopes = _alibi_slopes(H)
    atabv = np.zeros((H * NSLOT, 2048), dtype=np.float32)
    posB = np.array(B)
    for j in range(NSLOT):
        K = KCOLS[j]
        e_j = EJ[j]
        q0 = ql[j] * 128
        col = np.arange(K)
        kb = col // 128
        posk = posB[kb] * 128 + col % 128
        valid = (kb < e_j) & (posB[kb] <= ql[j])
        base = (posk - q0).astype(np.float32)
        for h in range(H):
            atabv[h * NSLOT + j, 0:K] = np.where(valid, slopes[h] * base,
                                                 -60000.0)
    return B, atabv.astype(np.float16)


def kernel(L0_attn_hidden_states, L0_attn_Wq, L0_attn_Wkv, L0_attn_Wo,
           L0_ffn_qweight, L0_ffn_weight_scale, L0_ffn_weight_zero_point,
           L0_ffn_bias):
    x = np.ascontiguousarray(np.asarray(L0_attn_hidden_states,
                                        dtype=np.float32))
    Wq = np.ascontiguousarray(np.asarray(L0_attn_Wq, dtype=np.float32))
    Wkv = np.ascontiguousarray(np.asarray(L0_attn_Wkv, dtype=np.float32))
    Wo = np.ascontiguousarray(np.asarray(L0_attn_Wo, dtype=np.float32))
    qwt = np.ascontiguousarray(np.asarray(L0_ffn_qweight, dtype=np.int32))
    fsc = np.ascontiguousarray(
        np.asarray(L0_ffn_weight_scale, dtype=np.float32).reshape(32, 128).T)
    fzp = np.ascontiguousarray(
        np.asarray(L0_ffn_weight_zero_point,
                   dtype=np.float32).reshape(32, 128).T)
    fb3 = np.zeros((3, 1536), dtype=np.float32)
    fb3.reshape(-1)[:D] = np.asarray(L0_ffn_bias, dtype=np.float32).reshape(-1)
    fbias = fb3

    tri = np.where(np.arange(128)[None, :] > np.arange(128)[:, None],
                   np.float32(NEG), np.float32(0.0)).astype(np.float32)
    idf = np.eye(128, dtype=np.float32)
    ones1 = np.ones((65, 128), dtype=np.float16)

    in_maps = []
    perms = []
    for c in range(8):
        b, half = c // 2, c % 2
        B, atabv = _host_tables(half)
        xbc = np.ascontiguousarray(
            x[b].reshape(NBLK, 128, D)[B].reshape(S, D))
        in_maps.append(dict(
            xb=xbc, wq=Wq, wkv=Wkv, wo=Wo, qw=qwt, fsc=fsc, fzp=fzp,
            fbias=fbias, atab=atabv, tri=tri, idf=idf, ones1=ones1))
        perms.append(QL[half])

    nc = build_program()
    res = run_bass_kernel_spmd(nc, in_maps, core_ids=list(range(8)))

    y = np.empty((BATCH, S, D), dtype=np.float32)
    for c in range(8):
        b = c // 2
        o = np.asarray(res.results[c]["out"]).reshape(NSLOT, 128, D)
        yb = y[b].reshape(NBLK, 128, D)
        for j, blk in enumerate(perms[c]):
            yb[blk] = o[j]
    return y


# version 22
# speedup vs baseline: 1.1719x; 1.1719x over previous
import numpy as np
import ml_dtypes
import concourse.bass as bass
import concourse.mybir as mybir
from concourse.bass_utils import run_bass_kernel_spmd

F32 = mybir.dt.float32
BF16 = mybir.dt.bfloat16
F16 = mybir.dt.float16
I32 = mybir.dt.int32
AF = mybir.ActivationFunctionType
ALU = mybir.AluOpType
AX = mybir.AxisListType
bfd = ml_dtypes.bfloat16

D = 4096
S = 2048
BATCH = 4
H = 32
NBLK = 16
NSLOT = 8
RQ = 1024
EPS = 1e-5
QSCALE = 128.0 ** -0.5
NEG = -1e30

EJ = [2 * (j + 1) for j in range(NSLOT)]
SJ = [(e + 3) // 4 for e in EJ]
KCOLS = [s * 512 for s in SJ]

QL = {0: [0, 3, 4, 7, 8, 11, 12, 15], 1: [1, 2, 5, 6, 9, 10, 13, 14]}

ENGS = ("sync", "gpsimd", "vector", "scalar", "tensor")
DMA_SEMS = ("QC", "X0", "X1", "XR", "A0", "A1", "A2", "A3", "A4", "A5",
            "W0", "W1",
            "WC", "QW0", "QW1", "QW2", "QW3", "S0", "S1", "TP")
SEM_AMT = {**{s: 16 for s in DMA_SEMS}, "PE": 1, "DV": 1, "AC": 1, "PO": 1}


def build_program(debug=False, race_check=True):
    nc = bass.Bass("TRN2", target_bir_lowering=False,
                   detect_race_conditions=race_check)

    xb = nc.dram_tensor("xb", [S, D], F32, kind="ExternalInput").ap()
    wq = nc.dram_tensor("wq", [D, D], F32, kind="ExternalInput").ap()
    wkv = nc.dram_tensor("wkv", [D, 256], F32, kind="ExternalInput").ap()
    wo = nc.dram_tensor("wo", [D, D], F32, kind="ExternalInput").ap()
    qw = nc.dram_tensor("qw", [D, 3072], I32, kind="ExternalInput").ap()
    fsc_d = nc.dram_tensor("fsc", [128, 32], F32, kind="ExternalInput").ap()
    fzp_d = nc.dram_tensor("fzp", [128, 32], F32, kind="ExternalInput").ap()
    fbias_d = nc.dram_tensor("fbias", [3, 1536], F32,
                             kind="ExternalInput").ap()
    atab = nc.dram_tensor("atab", [H * NSLOT, 2048], F16,
                          kind="ExternalInput").ap()
    tri_d = nc.dram_tensor("tri", [128, 128], F32, kind="ExternalInput").ap()
    idf_d = nc.dram_tensor("idf", [128, 128], F32, kind="ExternalInput").ap()
    ones_d = nc.dram_tensor("ones1", [65, 128], F16,
                            kind="ExternalInput").ap()
    out_d = nc.dram_tensor("out", [RQ, D], F32, kind="ExternalOutput").ap()

    q = {e: [] for e in ENGS}
    cnt = {s: 0 for s in SEM_AMT}
    last_wait = {e: {s: 0 for s in SEM_AMT} for e in ENGS}
    pending = {}

    def E(eng, fn, waits=(), inc=None):
        w = list(pending.pop(eng, {}).items()) + list(waits)
        lw = last_wait[eng]
        eff = []
        for s, v in w:
            if v is not None and v > lw[s]:
                lw[s] = v
                eff.append((s, v))
        m = None
        if inc is not None:
            cnt[inc] += SEM_AMT[inc]
            m = cnt[inc]
        q[eng].append((tuple(eff), fn, inc))
        return m

    def bar():
        snap = dict(cnt)
        for e in ENGS:
            p = pending.setdefault(e, {})
            for s, v in snap.items():
                if v > p.get(s, 0):
                    p[s] = v

    dbg_tensors = {}

    def tap(name, sb_t, shape, dt, waits=()):
        if not debug:
            return None
        d = nc.dram_tensor("dbg_" + name, shape, F32,
                           kind="ExternalOutput").ap()
        return E("gpsimd", (lambda d, sb_t: lambda e: e.dma_start(
            d, sb_t[:]))(d, sb_t), waits=waits, inc="TP")

    guards = {}

    uid = [0]

    def sopen(name, shape, dt, side="left"):
        uid[0] += 1
        g = nc.sbuf_tensor(f"{name}_u{uid[0]}", shape, dt, side=side)
        guards[name] = g
        return g.__enter__()

    def sclose(*names):
        for n in names:
            guards.pop(n).__exit__(None, None, None)

    idf = sopen("idf_sb", [128, 128], F32)
    idb = sopen("idb_sb", [128, 128], F16)
    ones1 = sopen("ones_sb", [65, 128], F16)
    fsc = sopen("fsc_sb", [128, 32], F32)
    fzp = sopen("fzp_sb", [128, 32], F32)
    nzs = fzp
    fbias = sopen("fbias_bf", [65, 1536], F16)
    epsb = sopen("epsb", [128, 1], F32)
    kT = sopen("kT", [128, 2048], F16)
    vT = sopen("vT", [128, NBLK * 128], F16)
    tri = sopen("tri_sb", [128, 128], F32)

    psA_g = nc.psum_tensor("psA", [128, 2048], F32)
    psA = psA_g.__enter__()
    psB_g = nc.psum_tensor("psB", [128, 2048], F32)
    psB = psB_g.__enter__()

    sems = {}
    sem_guards = []
    for s in SEM_AMT:
        g = nc.semaphore("sem" + s)
        sems[s] = g.__enter__()
        sem_guards.append(g)

    E("sync", lambda e: e.dma_start(tri[:], tri_d), inc="QC")
    E("sync", lambda e: e.dma_start(idf[:], idf_d), inc="QC")
    E("sync", lambda e: e.dma_start(ones1[:], ones_d), inc="QC")
    E("sync", lambda e: e.dma_start(fsc[:], fsc_d), inc="QC")
    m_cl = E("sync", lambda e: e.dma_start(fzp[:], fzp_d), inc="QC")
    E("vector", lambda e: e.tensor_copy(idb[:], idf[:]), waits=[("QC", m_cl)])
    E("vector", lambda e: e.memset(epsb[:], EPS))
    m_consts = E("vector", lambda e: e.scalar_tensor_tensor(
        fzp[:], fzp[:], -1.0, fsc[:], ALU.mult, ALU.mult), inc="DV")

    ln1T = sopen("ln1T", [128, 32 * 1024], F16, side="right")
    wkv_bf = sopen("wkv_bf", [128, 32 * 256], F16, side="right")
    xrt = [sopen(f"xrt{i}", [128, D], F32) for i in range(2)]
    lnbf = [sopen(f"lnbf{i}", [128, D], F16) for i in range(2)]
    scr_s = sopen("scr_s", [128, D], F32)
    scr_q = sopen("scr_q", [128, D], F32)
    roll = sopen("roll", [128, 32 * 128], F16)
    kvbf = [sopen(f"kvbf{i}", [128, 256], F16) for i in range(2)]
    st = {n: sopen("st_" + n, [128, NBLK], F32)
          for n in ("sum", "ssq", "mean", "msq", "var", "lnv", "rstd", "nmr")}

    for r3 in range(3):
        m_fb = E("gpsimd", (lambda r3: lambda e: e.dma_start(
            fbias[32 * r3:32 * r3 + 1, :], fbias_d[r3:r3 + 1, :]))(r3),
            inc="WC")
    m_wkv = E("gpsimd", lambda e: e.dma_start(
        wkv_bf[:].rearrange("p (s c) -> p s c", s=32),
        wkv.rearrange("(s p) c -> p s c", p=128)), inc="WC")

    psKV = [psA[:, 0:256], psA[:, 512:768]]
    psT = [psB[:, 0:512], psB[:, 512:1024]]
    psKT = [psB[:, 1024:1152], psB[:, 1536:1664]]

    m_x = {}
    m_apply = {}
    prev_m1 = [None, None]
    m_tev = {}
    m_kv = {}
    m_kvev = {}
    m_ktev = {}
    tb = 0

    def xdma(rt, wl):
        m_x[rt] = E("sync", (lambda r: lambda e: e.dma_start(
            xrt[r % 2][:], xb[r * 128:(r + 1) * 128, :]))(rt),
            waits=wl, inc="X%d" % (rt % 2))

    xdma(0, [])
    xdma(1, [])

    def emit_stats(rt):
        nonlocal_ = prev_m1
        r2 = rt % 2
        wl0 = [("X%d" % (rt % 2), m_x[rt])]
        if prev_m1[0] is not None:
            wl0.append(("DV", prev_m1[1]))
        m1 = E("scalar", (lambda r, r2: lambda e: e.activation(
            scr_s[:], xrt[r2][:], AF.Identity, bias=0.0, scale=1.0,
            accum_out=st["sum"][:, r:r + 1]))(rt, r2),
            waits=wl0, inc="AC")
        m2 = E("vector", (lambda r, r2: lambda e: e.scalar_tensor_tensor(
            scr_q[:], xrt[r2][:], 1.0, xrt[r2][:], ALU.mult, ALU.mult,
            accum_out=st["ssq"][:, r:r + 1]))(rt, r2), waits=wl0, inc="DV")
        prev_m1[0], prev_m1[1] = m1, m2
        m3 = E("vector", (lambda r: lambda e: e.tensor_scalar(
            st["mean"][:, r:r + 1], st["sum"][:, r:r + 1], 1.0 / D, None,
            ALU.mult))(rt), waits=[("AC", m1)], inc="DV")
        m4 = E("vector", (lambda r: lambda e: e.tensor_tensor(
            st["msq"][:, r:r + 1], st["mean"][:, r:r + 1],
            st["mean"][:, r:r + 1], ALU.mult))(rt), waits=[("DV", m3)],
            inc="DV")
        m_var = E("vector", (lambda r: lambda e: e.scalar_tensor_tensor(
            st["var"][:, r:r + 1], st["ssq"][:, r:r + 1], 1.0 / D,
            st["msq"][:, r:r + 1], ALU.mult, ALU.subtract))(rt),
            waits=[("DV", m2), ("DV", m4)], inc="DV")
        m_lnv = E("scalar", (lambda r: lambda e: e.activation(
            st["lnv"][:, r:r + 1], st["var"][:, r:r + 1], AF.Ln, bias=epsb[:],
            scale=1.0))(rt), waits=[("DV", m_var)], inc="AC")
        m_rstd = E("scalar", (lambda r: lambda e: e.activation(
            st["rstd"][:, r:r + 1], st["lnv"][:, r:r + 1], AF.Exp, bias=0.0,
            scale=-0.5))(rt), waits=[("AC", m_lnv)], inc="AC")
        m_nmr = E("vector", (lambda r: lambda e: e.scalar_tensor_tensor(
            st["nmr"][:, r:r + 1], st["mean"][:, r:r + 1], -1.0,
            st["rstd"][:, r:r + 1], ALU.mult, ALU.mult))(rt),
            waits=[("AC", m_rstd)], inc="DV")
        wl = [("DV", m_nmr)]
        if rt - 2 in m_kv:
            wl.append(("PE", m_kv[rt - 2]))
        m_apply[rt] = E("scalar", (lambda r, r2: lambda e: e.activation(
            lnbf[r2][:], xrt[r2][:], AF.Identity, bias=st["nmr"][:, r:r + 1],
            scale=st["rstd"][:, r:r + 1]))(rt, r2), waits=wl, inc="AC")

        if rt + 2 < NBLK:
            xdma(rt + 2, [("AC", m_apply[rt])])

    emit_stats(0)
    for rt in range(NBLK):
        r2 = rt % 2
        if rt + 1 < NBLK:
            emit_stats(rt + 1)
        odd = (rt % 2 == 1)
        j = rt // 2
        rt_tb0 = tb
        for bt in range(8):
            wl = [("AC", m_apply[rt])]
            if tb - 2 in m_tev:
                wl.append(("DV", m_tev[tb - 2]))
            m = None
            for i in range(4):
                kc = bt * 4 + i
                m = E("tensor", (lambda r2, kc, t2, i: lambda e: e.matmul(
                    psT[t2][:, i * 128:(i + 1) * 128],
                    lnbf[r2][:, kc * 128:(kc + 1) * 128], idb[:],
                    start=True, stop=True))(r2, kc, tb % 2, i),
                    waits=wl if i == 0 else (), inc="PE" if i == 3 else None)
            if odd:
                def mk_ev(jj, bt, t2):
                    def f(e):
                        d = ln1T[:].rearrange("p (kc r) -> p kc r", r=1024)
                        d = d[:, bt * 4:(bt + 1) * 4,
                              jj * 128:jj * 128 + 128]
                        return e.tensor_copy(d, psT[t2][:].rearrange(
                            "p (k c) -> p k c", k=4))
                    return f
                m_tev[tb] = E("vector", mk_ev(j, bt, tb % 2),
                              waits=[("PE", m)], inc="DV")
            else:
                m_tev[tb] = E("vector", (lambda bt, t2: lambda e:
                                         e.tensor_copy(
                    roll[:, bt * 512:(bt + 1) * 512], psT[t2][:]))(bt, tb % 2),
                    waits=[("PE", m)], inc="DV")
            tb += 1

        if odd:
            src = (lambda jj: lambda kc: ln1T[
                :, kc * 1024 + jj * 128: kc * 1024 + jj * 128 + 128])(j)
        else:
            src = lambda kc: roll[:, kc * 128:(kc + 1) * 128]
        mkv = None
        for kc in range(32):
            wl = []
            if kc % 4 == 0:
                wl.append(("DV", m_tev[rt_tb0 + kc // 4]))
            if kc == 0:
                wl.append(("WC", m_wkv))
                if rt - 2 in m_kvev:
                    wl.append(("DV", m_kvev[rt - 2]))
            mkv = E("tensor", (lambda kc, r2, src: lambda e: e.matmul(
                psKV[r2], src(kc), wkv_bf[:, kc * 256:(kc + 1) * 256],
                start=(kc == 0), stop=(kc == 31)))(kc, r2, src),
                waits=wl, inc="PE" if kc == 31 else None)
        m_kv[rt] = mkv
        m_kvev[rt] = E("vector", (lambda r2: lambda e: e.tensor_copy(
            kvbf[r2][:], psKV[r2]))(r2), waits=[("PE", m_kv[rt])], inc="DV")
        E("vector", (lambda r, r2: lambda e: e.tensor_copy(
            vT[:, r * 128:(r + 1) * 128], kvbf[r2][:, 128:256]))(rt, r2),
            waits=[("DV", m_kvev[rt])])
        wl = [("DV", m_kvev[rt])]
        if rt - 2 in m_ktev:
            wl.append(("DV", m_ktev[rt - 2]))
        m_kt = E("tensor", (lambda r2: lambda e: e.matmul(
            psKT[r2], kvbf[r2][:, 0:128], idb[:], start=True,
            stop=True))(r2), waits=wl, inc="PE")
        m_ktev[rt] = E("vector", (lambda r, r2: lambda e: e.tensor_copy(
            kT[:, r * 128:(r + 1) * 128], psKT[r2]))(rt, r2),
            waits=[("PE", m_kt)], inc="DV")

    bar()
    tap("kT", kT, [128, 2048], F16)
    tap("vT", vT, [128, NBLK * 128], F16)
    tap("ln1T", ln1T, [128, 32 * 1024], F16)
    bar()
    sclose(*("st_" + n for n in reversed(("sum", "ssq", "mean", "msq", "var",
                                          "lnv", "rstd", "nmr"))))
    sclose("kvbf1", "kvbf0", "roll", "scr_q", "scr_s", "lnbf1", "lnbf0",
           "xrt1", "xrt0")
    sclose("wkv_bf")

    qT = sopen("qT", [128, 32 * 1024], F16)
    wqb = [sopen(f"wqb{i}", [128, 8 * 512], F16) for i in range(2)]


    psWq = [psA[:, t * 512:(t + 1) * 512] for t in range(4)] + \
           [psB[:, t * 512:(t + 1) * 512] for t in range(4)]

    m_wq_done = {}
    m_wqc = {}
    m_wqev = {}

    def wq_dma(ci, wl):
        ntg, kcs = ci // 4, ci % 4
        m_wqc[ci] = E("gpsimd", (lambda ntg, kcs, ci: lambda e: e.dma_start(
            wqb[ci % 2][:].rearrange("p (s c) -> p s c", s=8),
            wq[kcs * 1024:(kcs + 1) * 1024,
               ntg * 512:(ntg + 1) * 512].rearrange(
                "(s p) c -> p s c", p=128)))(ntg, kcs, ci),
            waits=wl, inc="W%d" % (ci % 2))

    wq_dma(0, [])
    wq_dma(1, [])
    for ntg in range(8):
        for kcs in range(4):
            ci = ntg * 4 + kcs
            mlast = None
            for kc8 in range(8):
                kc = kcs * 8 + kc8
                for i in range(4):
                    for rs in range(2):
                        wl = []
                        if kc8 == 0 and i == 0 and rs == 0:
                            wl.append(("W%d" % (ci % 2), m_wqc[ci]))
                            if kc == 0 and ntg - 1 in m_wqev:
                                wl.append(("DV", m_wqev[ntg - 1]))
                        mlast = E("tensor", (lambda ci, kc8, i, rs, kc:
                                             lambda e: e.matmul(
                            psWq[i * 2 + rs],
                            wqb[ci % 2][:, kc8 * 512 + i * 128:
                                        kc8 * 512 + i * 128 + 128],
                            ln1T[:, kc * 1024 + rs * 512:
                                 kc * 1024 + rs * 512 + 512],
                            start=(kc == 0),
                            stop=(kc == 31)))(ci, kc8, i, rs, kc),
                            waits=wl,
                            inc="PE" if (kc8 == 7 and i == 3 and rs == 1)
                            else None)
            m_wq_done[ci] = mlast
            if ci + 2 < 32:
                wq_dma(ci + 2, [("PE", m_wq_done[ci])])
        for i in range(4):
            for rs in range(2):
                h = ntg * 4 + i
                mm = E("vector", (lambda h, rs, i: lambda e: e.tensor_scalar(
                    qT[:, h * 1024 + rs * 512: h * 1024 + rs * 512 + 512],
                    psWq[i * 2 + rs], QSCALE, None, ALU.mult))(h, rs, i),
                    waits=[("PE", m_wq_done[ntg * 4 + 3])]
                    if (i == 0 and rs == 0) else (),
                    inc="DV" if (i == 3 and rs == 1) else None)
        m_wqev[ntg] = mm

    bar()
    tap("qT", qT, [128, 32 * 1024], F16)
    bar()
    sclose("wqb1", "wqb0")
    sclose("ln1T")

    AVT_B = sopen("AVT_B", [128, 32 * 512], F16, side="right")
    AVT_A = sopen("AVT_A", [128, 32 * 512], F16, side="right")
    attn_sb = [sopen(f"attn_sb{i}", [128, 4 * 2048], F16) for i in range(2)]
    diagm = [sopen(f"diagm{i}", [128, 4 * 128], F16) for i in range(2)]
    atst = [sopen(f"atst{i}", [1, 2048], F16) for i in range(6)]
    attnT = [sopen(f"attnT{i}", [128, 512], F16) for i in range(2)]
    lbuf = sopen("lbuf", [128, 4], F32)
    linv = sopen("linv", [128, 4], F32)
    negmax = sopen("negmax", [128, 2], F32)
    pmax = sopen("pmax", [128, 16], F32)

    psT2 = [psB[:, 0:512], psB[:, 512:1024]]
    psAVT = [psB[:, 1024:1536], psB[:, 1536:2048]]

    m_ali = {}
    m_rank1 = {}
    m_exp = {}
    m_Btr = {}
    m_Tev2 = {}
    m_AVTev = {}
    m_diag = {}
    bstate = {}
    a_ctr = 0

    def prev2B(j, g):
        if g >= 2:
            return (j, g - 2)
        if j >= 1:
            return (j - 1, g + 6)
        return None

    def emit_B_chunk(bj, bg, kbs):
        e_b = EJ[bj]
        st = bstate.setdefault((bj, bg), {})

        def emit_av(kb):
            wl = [("DV", m_Tev2[(bj, bg, kb)])]
            if kb == 0:
                pk = prev2B(bj, bg)
                if pk in m_AVTev:
                    wl.append(("DV", m_AVTev[pk]))
            st["m_av"] = E("tensor", (lambda kb, bg, e_b: lambda e: e.matmul(
                psAVT[bg % 2], vT[:, kb * 128:(kb + 1) * 128],
                attnT[kb % 2][:],
                start=(kb == 0), stop=(kb == e_b - 1)))(kb, bg, e_b),
                waits=wl, inc="PE" if kb == e_b - 1 else None)

        for kb in kbs:
            wl = []
            if (bj, bg, kb - 2) in m_Tev2:
                wl.append(("DV", m_Tev2[(bj, bg, kb - 2)]))
            if kb == 0:
                wl.append(("AC", m_diag[(bj, bg)]))
            mT = None
            for hh in range(4):
                mT = E("tensor", (lambda kb, hh, bg: lambda e: e.matmul(
                    psT2[kb % 2][:, hh * 128:(hh + 1) * 128],
                    attn_sb[bg % 2][:, hh * 2048 + kb * 128:
                                    hh * 2048 + kb * 128 + 128],
                    diagm[bg % 2][:, hh * 128:(hh + 1) * 128],
                    start=True, stop=True))(kb, hh, bg),
                    waits=wl if hh == 0 else (),
                    inc="PE" if hh == 3 else None)
            m_Tev2[(bj, bg, kb)] = E("vector", (lambda kb: lambda e:
                                               e.tensor_copy(
                attnT[kb % 2][:], psT2[kb % 2][:]))(kb),
                waits=[("PE", mT)], inc="DV")
            if "pend" in st:
                emit_av(st.pop("pend"))
            st["pend"] = kb
        if kbs and kbs[-1] == e_b - 1:
            emit_av(st.pop("pend"))
            m_Btr[(bj, bg)] = st["m_av"]
            AVT = AVT_A if bj < 4 else AVT_B
            jw = bj % 4

            def mk_avtev(AVT, bg, jw):
                def f(e):
                    d = AVT[:].rearrange("p (h r) -> p h r", r=512)
                    d = d[:, bg * 4:(bg + 1) * 4, jw * 128: jw * 128 + 128]
                    return e.tensor_copy(d, psAVT[bg % 2].rearrange(
                        "p (h c) -> p h c", h=4))
                return f
            m_AVTev[(bj, bg)] = E("vector", mk_avtev(AVT, bg, jw),
                                  waits=[("PE", st["m_av"])], inc="DV")

    prevB = None
    for j in range(NSLOT):
        K = KCOLS[j]
        s = SJ[j]
        e_j = EJ[j]
        depth2 = K <= 1024
        for g in range(8):
            if prevB is not None:
                pe_b = EJ[prevB[0]]
                kbs_all = list(range(pe_b))
                chunks = [kbs_all[(c * pe_b) // 4:((c + 1) * pe_b) // 4]
                          for c in range(4)]
            else:
                chunks = [[], [], [], []]
            for hh in range(4):
                h = g * 4 + hh
                a = a_ctr
                a_ctr += 1
                wl = []
                if a - 6 >= 0:
                    wl.append(("PE", m_rank1[a - 6]))
                m_ali[a] = E("sync", (lambda j, h, a, K: lambda e:
                                      e.dma_start(
                    atst[a % 6][0:1, 0:K],
                    atab[h * NSLOT + j: h * NSLOT + j + 1, 0:K]))(
                        j, h, a, K),
                    waits=wl, inc="A%d" % (a % 6))

                if depth2:
                    roff = (hh % 2) * 1024
                    prev = (j, g, hh - 2) if hh >= 2 else                         ((j, g - 1, hh + 2) if g >= 1 else
                         ((j - 1, 7, hh + 2) if j >= 1 else None))
                else:
                    roff = 0
                    prev = (j, g, hh - 1) if hh >= 1 else                         ((j, g - 1, 3) if g >= 1 else
                         ((j - 1, 7, 3) if j >= 1 else None))
                wl = [("A%d" % (a % 6), m_ali[a])]
                if prev in m_exp:
                    wl.append(("AC", m_exp[prev]))
                mr = None
                for i in range(s):
                    mr = E("tensor", (lambda roff, i, a: lambda e: e.matmul(
                        psA[:, roff + i * 512: roff + (i + 1) * 512],
                        ones1[0:1, :],
                        atst[a % 6][0:1, i * 512:(i + 1) * 512],
                        start=True, stop=False))(roff, i, a),
                        waits=wl if i == 0 else (),
                        inc="PE" if i == s - 1 else None)
                m_rank1[a] = mr
                mqk = []
                for i in range(s):
                    mqk.append(E("tensor", (lambda roff, i, h, j: lambda e:
                                            e.matmul(
                        psA[:, roff + i * 512: roff + (i + 1) * 512],
                        qT[:, h * 1024 + j * 128: h * 1024 + j * 128 + 128],
                        kT[:, i * 512:(i + 1) * 512],
                        start=False, stop=True))(roff, i, h, j), inc="PE"))
                dslab = (2 * j + 1) // 4
                m_tri = E("vector", (lambda roff, j: lambda e:
                                     e.tensor_tensor(
                    psA[:, roff + (2 * j + 1) * 128:
                        roff + (2 * j + 2) * 128],
                    psA[:, roff + (2 * j + 1) * 128:
                        roff + (2 * j + 2) * 128], tri[:],
                    ALU.add))(roff, j),
                    waits=[("PE", mqk[dslab])], inc="DV")
                m_pm = None
                for i in range(s):
                    wlp = [("PE", mqk[i])]
                    if i == dslab:
                        wlp.append(("DV", m_tri))
                    m_pm = E("vector", (lambda roff, i, hh: lambda e:
                                        e.reduce_max(
                        pmax[:, hh * 4 + i:hh * 4 + i + 1],
                        psA[:, roff + i * 512: roff + (i + 1) * 512],
                        AX.X, negate=True))(roff, i, hh),
                        waits=wlp, inc="DV")
                m_cmb = E("vector", (lambda hh, s: lambda e: e.tensor_reduce(
                    negmax[:, hh % 2:hh % 2 + 1],
                    pmax[:, hh * 4:hh * 4 + s], AX.X, ALU.min))(hh, s),
                    waits=[("DV", m_pm)], inc="DV")
                wl2 = [("DV", m_cmb)]
                pb = prev2B(j, g)
                if pb in m_Btr:
                    wl2.append(("PE", m_Btr[pb]))
                m_exp[(j, g, hh)] = E("scalar", (lambda roff, K, hh, g:
                                                 lambda e: e.activation(
                    attn_sb[g % 2][:, hh * 2048: hh * 2048 + K],
                    psA[:, roff:roff + K],
                    AF.Exp, bias=negmax[:, hh % 2:hh % 2 + 1], scale=1.0,
                    accum_out=lbuf[:, hh:hh + 1]))(roff, K, hh, g),
                    waits=wl2, inc="AC")
                if prevB is not None:
                    emit_B_chunk(prevB[0], prevB[1], chunks[hh])
            wl = [("AC", m_exp[(j, g, 3)])]
            m_rcp = E("vector", lambda e: e.reciprocal(linv[:], lbuf[:]),
                      waits=wl, inc="DV")
            pk = prev2B(j, g)
            md = None
            for hh in range(4):
                wl = [("DV", m_rcp)] if hh == 0 else ()
                if hh == 0 and pk in m_Btr:
                    wl.append(("PE", m_Btr[pk]))
                md = E("scalar", (lambda hh, g: lambda e: e.activation(
                    diagm[g % 2][:, hh * 128:(hh + 1) * 128], idf[:],
                    AF.Identity, bias=0.0,
                    scale=linv[:, hh:hh + 1]))(hh, g),
                    waits=wl, inc="AC" if hh == 3 else None)
            m_diag[(j, g)] = md
            prevB = (j, g)
    emit_B_chunk(prevB[0], prevB[1], list(range(EJ[prevB[0]])))

    bar()
    tap("AVT_A", AVT_A, [128, 32 * 512], F16)
    tap("AVT_B", AVT_B, [128, 32 * 512], F16)
    bar()
    sclose("pmax", "negmax", "linv", "lbuf", "attnT1", "attnT0",
           "atst5", "atst4", "atst3", "atst2", "atst1", "atst0",
           "diagm1", "diagm0",
           "attn_sb1", "attn_sb0", "qT",
           "tri_sb", "vT", "kT")

    x2s = sopen("x2s", [128, 4 * D], F32)
    ln2Ts = [sopen("ln2T0", [128, 32 * 512], F16)]
    x2sp = nc.dram_tensor("x2sp", [512, D], F32, kind="Internal").ap()
    m_l2dones = []
    m_spill = None
    m_spills = []

    psWo = [psA[:, t * 512:(t + 1) * 512] for t in range(4)] + \
           [psB[:, t * 512:(t + 1) * 512] for t in range(4)]

    for slab in range(2):
        AVT = AVT_A if slab == 0 else AVT_B
        if slab == 1:
            ln2Ts.append(sopen("ln2T1", [128, 32 * 512], F16))
        xres = [sopen("xres0", [128, D], F32)]
        xres = [xres[0], xres[0]]
        wob = [sopen(f"wob{i}", [128, 8 * 512], F16) for i in range(2)]

        m_woc = {}
        m_wo_done = {}
        m_woev = {}

        def wo_dma(ci, wl):
            nt, kcs = ci // 4, ci % 4
            m_woc[ci] = E("gpsimd", (lambda nt, kcs, ci, wob=wob: lambda e:
                                     e.dma_start(
                wob[ci % 2][:].rearrange("p (s c) -> p s c", s=8),
                wo[kcs * 1024:(kcs + 1) * 1024,
                   nt * 512:(nt + 1) * 512].rearrange(
                    "(s p) c -> p s c", p=128)))(nt, kcs, ci),
                waits=wl, inc="W%d" % (ci % 2))

        wo_dma(0, [])
        wo_dma(1, [])
        for nt in range(8):
            for kcs in range(4):
                ci = nt * 4 + kcs
                mlast = None
                for kc8 in range(8):
                    kc = kcs * 8 + kc8
                    for rtw in range(4):
                        wl = []
                        if kc8 == 0 and rtw == 0:
                            wl.append(("W%d" % (ci % 2), m_woc[ci]))
                            if kc == 0 and nt - 2 in m_woev:
                                wl.append(("DV", m_woev[nt - 2]))
                        mlast = E("tensor", (lambda ci, kc8, kc, rtw, nt,
                                             AVT=AVT, wob=wob:
                                             lambda e: e.matmul(
                            psWo[(nt % 2) * 4 + rtw],
                            AVT[:, kc * 512 + rtw * 128:
                                kc * 512 + rtw * 128 + 128],
                            wob[ci % 2][:, kc8 * 512:(kc8 + 1) * 512],
                            start=(kc == 0),
                            stop=(kc == 31)))(ci, kc8, kc, rtw, nt),
                            waits=wl,
                            inc="PE" if (kc8 == 7 and rtw == 3) else None)
                m_wo_done[ci] = mlast
                if ci + 2 < 32:
                    wo_dma(ci + 2, [("PE", m_wo_done[ci])])
            mm = None
            for rtw in range(4):
                wl = []
                if rtw == 0:
                    wl = [("PE", m_wo_done[nt * 4 + 3])]
                    if slab == 1 and nt == 0:
                        wl.append(("S1", m_spill))
                mm = E("vector", (lambda nt, rtw: lambda e: e.tensor_copy(
                    x2s[:, rtw * D + nt * 512: rtw * D + nt * 512 + 512],
                    psWo[(nt % 2) * 4 + rtw]))(nt, rtw),
                    waits=wl,
                    inc="DV" if rtw == 3 else None)
            m_woev[nt] = mm

        m_xr = {}
        m_res = {}

        def xr_dma(rtw, wl):
            j = slab * 4 + rtw
            m_xr[rtw] = E("sync", (lambda j, rtw, xres=xres: lambda e:
                e.dma_start(
                xres[rtw % 2][:],
                xb[(2 * j + 1) * 128:(2 * j + 2) * 128, :]))(j, rtw),
                waits=wl, inc="XR")

        sclose("wob1", "wob0")
        sclose("AVT_A" if slab == 0 else "AVT_B")
        xr_dma(0, [])

        lnbf2 = [sopen(f"lnbf2_{i}", [128, D], F16) for i in range(2)]
        scr2s = sopen("scr2s", [128, D], F32)
        scr2q = sopen("scr2q", [128, D], F32)
        s2 = {n: sopen("s2_" + n, [128, 4], F32)
              for n in ("sum", "ssq", "mean", "msq", "var", "lnv", "rstd",
                        "nmr")}

        m_l2ap = {}
        m_l2tev = {}
        m_l2kv = {}
        tb3 = 0

        def emit_l2s(rtw):
            m_res[rtw] = E("vector", (lambda rtw, xres=xres: lambda e:
                e.tensor_tensor(
                x2s[:, rtw * D:(rtw + 1) * D], x2s[:, rtw * D:(rtw + 1) * D],
                xres[rtw % 2][:], ALU.add))(rtw),
                waits=[("XR", m_xr[rtw]), ("DV", m_woev[7])], inc="DV")
            if rtw + 1 < 4:
                xr_dma(rtw + 1, [("DV", m_res[rtw])])
            wl0 = [("DV", m_res[rtw])]
            m1 = E("scalar", (lambda rtw, scr2s=scr2s, s2=s2: lambda e:
                e.activation(
                scr2s[:], x2s[:, rtw * D:(rtw + 1) * D], AF.Identity,
                bias=0.0, scale=1.0,
                accum_out=s2["sum"][:, rtw:rtw + 1]))(rtw),
                waits=wl0, inc="AC")
            m2 = E("vector", (lambda rtw, scr2q=scr2q, s2=s2: lambda e:
                e.scalar_tensor_tensor(
                scr2q[:], x2s[:, rtw * D:(rtw + 1) * D], 1.0,
                x2s[:, rtw * D:(rtw + 1) * D], ALU.mult, ALU.mult,
                accum_out=s2["ssq"][:, rtw:rtw + 1]))(rtw), inc="DV")
            m3 = E("vector", (lambda rtw, s2=s2: lambda e: e.tensor_scalar(
                s2["mean"][:, rtw:rtw + 1], s2["sum"][:, rtw:rtw + 1],
                1.0 / D, None, ALU.mult))(rtw), waits=[("AC", m1)],
                inc="DV")
            m4 = E("vector", (lambda rtw, s2=s2: lambda e: e.tensor_tensor(
                s2["msq"][:, rtw:rtw + 1], s2["mean"][:, rtw:rtw + 1],
                s2["mean"][:, rtw:rtw + 1], ALU.mult))(rtw),
                waits=[("DV", m3)], inc="DV")
            m_var = E("vector", (lambda rtw, s2=s2: lambda e:
                e.scalar_tensor_tensor(
                s2["var"][:, rtw:rtw + 1], s2["ssq"][:, rtw:rtw + 1],
                1.0 / D, s2["msq"][:, rtw:rtw + 1], ALU.mult,
                ALU.subtract))(rtw), waits=[("DV", m2), ("DV", m4)],
                inc="DV")
            m_lnv = E("scalar", (lambda rtw, s2=s2: lambda e: e.activation(
                s2["lnv"][:, rtw:rtw + 1], s2["var"][:, rtw:rtw + 1], AF.Ln,
                bias=epsb[:], scale=1.0))(rtw), waits=[("DV", m_var)],
                inc="AC")
            m_rstd = E("scalar", (lambda rtw, s2=s2: lambda e: e.activation(
                s2["rstd"][:, rtw:rtw + 1], s2["lnv"][:, rtw:rtw + 1],
                AF.Exp, bias=0.0, scale=-0.5))(rtw), waits=[("AC", m_lnv)],
                inc="AC")
            m_nmr = E("vector", (lambda rtw, s2=s2: lambda e:
                                 e.scalar_tensor_tensor(
                s2["nmr"][:, rtw:rtw + 1], s2["mean"][:, rtw:rtw + 1], -1.0,
                s2["rstd"][:, rtw:rtw + 1], ALU.mult, ALU.mult))(rtw),
                waits=[("AC", m_rstd)], inc="DV")
            wl = [("DV", m_nmr)]
            if rtw - 2 in m_l2kv:
                wl.append(("PE", m_l2kv[rtw - 2]))
            m_l2ap[rtw] = E("scalar", (lambda rtw, lnbf2=lnbf2, s2=s2:
                lambda e: e.activation(
                lnbf2[rtw % 2][:], x2s[:, rtw * D:(rtw + 1) * D], AF.Identity,
                bias=s2["nmr"][:, rtw:rtw + 1],
                scale=s2["rstd"][:, rtw:rtw + 1]))(rtw), waits=wl, inc="AC")
            if slab == 0:
                m_sp = E("sync", (lambda rtw: lambda e: e.dma_start(
                    x2sp[rtw * 128:(rtw + 1) * 128, :],
                    x2s[:, rtw * D:(rtw + 1) * D]))(rtw),
                    waits=[("AC", m_l2ap[rtw])], inc="S1")
                m_spills.append(m_sp)

        emit_l2s(0)
        for rtw in range(4):
            if rtw + 1 < 4:
                emit_l2s(rtw + 1)
            mT = None
            for bt in range(8):
                wl = [("AC", m_l2ap[rtw])]
                if tb3 - 2 in m_l2tev:
                    wl.append(("DV", m_l2tev[tb3 - 2]))
                for i in range(4):
                    kc = bt * 4 + i
                    mT = E("tensor", (lambda rtw, kc, t3, i, lnbf2=lnbf2:
                                      lambda e:
                                      e.matmul(
                        psT[t3][:, i * 128:(i + 1) * 128],
                        lnbf2[rtw % 2][:, kc * 128:(kc + 1) * 128], idb[:],
                        start=True, stop=True))(rtw, kc, tb3 % 2, i),
                        waits=wl if i == 0 else (),
                        inc="PE" if i == 3 else None)

                def mk_l2ev(rtw, bt, t3, slab=slab):
                    def f(e):
                        d = ln2Ts[slab][:].rearrange(
                            "p (kc r) -> p kc r", r=512)
                        d = d[:, bt * 4:(bt + 1) * 4,
                              rtw * 128: rtw * 128 + 128]
                        return e.tensor_copy(d, psT[t3][:].rearrange(
                            "p (k c) -> p k c", k=4))
                    return f
                m_l2tev[tb3] = E("vector", mk_l2ev(rtw, bt, tb3 % 2),
                                 waits=[("PE", mT)], inc="DV")
                tb3 += 1
            m_l2kv[rtw] = mT
        if slab == 0:
            m_spill = m_spills[-1]
        m_l2dones.append(m_l2tev[tb3 - 1])
        if slab == 0:
            bar()
            tap("x2s", x2s, [128, 4 * D], F32)
            tap("ln2T", ln2Ts[0], [128, 32 * 512], F16)
            bar()

        sclose(*("s2_" + n for n in reversed(("sum", "ssq", "mean", "msq",
                                              "var", "lnv", "rstd", "nmr"))))
        sclose("scr2q", "scr2s", "lnbf2_1", "lnbf2_0", "xres0")

    qwb = [sopen(f"qwb{i}", [128, 768], I32) for i in range(3)]
    t0b = [sopen(f"t0b{i}", [128, 768], I32) for i in range(2)]
    t1b = [t0b[0][:, 512:768], t0b[1][:, 512:768]]
    ubf = [sopen(f"ubf{i}", [128, 1024], F16) for i in range(2)]
    wbf = [sopen(f"wbf{i}", [128, D], F16) for i in range(2)]
    wTb = [sopen(f"wTb{i}", [128, 32 * 256], F16) for i in range(2)]
    outsb = sopen("outsb", [128, 4 * 256], F32)
    x2rel = sopen("x2rel0", [128, 4 * 256], F32)

    m_qwc = {}
    m_unp = {}
    m_wtr_nti = {}
    m_ffn_done = {}
    m_fev = {}
    m_store = {}
    m_wtev = {}
    m_x2r = {}
    wtb = 0

    def qw_dma(hc, wl):
        nti, qtr = hc // 4, hc % 4
        m_qwc[hc] = E("sync", (lambda nti, qtr, hc, qwb=qwb: lambda e:
                               e.dma_start(
            qwb[hc % 3][:],
            qw[nti * 128:(nti + 1) * 128,
               qtr * 768:(qtr + 1) * 768]))(nti, qtr, hc),
            waits=wl, inc="QW%d" % (hc % 3))

    def x2r_dma(grp, wl):
        ms = None
        for t in range(4):
            ms = E("scalar", (lambda grp, t: lambda e: e.dma_start(
                x2rel[:, t * 256:(t + 1) * 256],
                x2sp[t * 128:(t + 1) * 128,
                     grp * 256:(grp + 1) * 256]))(grp, t),
                waits=wl if t == 0 else (), inc="X0")
        m_x2r[grp] = ms

    for i in range(3):
        qw_dma(i, [("DV", m_l2dones[1])])
    x2r_dma(0, [("S1", m_spill)])
    m_deq_by_hc = {}
    m_deqq = {}

    def emit_unpack(grp):
        for sub in range(2):
            nti = grp * 2 + sub
            for qtr in range(4):
                hc = nti * 4 + qtr
                eng = "vector"
                esem = "DV"
                e2 = hc % 2
                wl = [("QW%d" % (hc % 3), m_qwc[hc])]
                if hc - 2 in m_deq_by_hc:
                    wl.append(("AC", m_deq_by_hc[hc - 2]))
                if hc - 2 in m_unp:
                    wl.append((esem, m_unp[hc - 2]))
                m_t0 = E(eng, (lambda e2, t0b=t0b, qwb=qwb, hc=hc:
                               lambda e:
                               e.tensor_scalar(
                    t0b[e2][:], qwb[hc % 3][:], 63, None,
                    ALU.bitwise_and))(e2),
                    waits=wl, inc=esem)
                E(eng, (lambda e2, ubf=ubf, t0b=t0b: lambda e:
                    e.tensor_copy(
                    ubf[e2][:].rearrange("p (g f) -> p g f", f=4)[
                        :, :, 0:1],
                    t0b[e2][:].rearrange("p (g t) -> p g t", t=3)[
                        :, :, 0:1]))(e2),
                    waits=[(esem, m_t0)], inc=esem)
                E(eng, (lambda e2, ubf=ubf, t0b=t0b: lambda e:
                    e.tensor_copy(
                    ubf[e2][:].rearrange("p (g f) -> p g f", f=4)[
                        :, :, 2:3],
                    t0b[e2][:].rearrange("p (g t) -> p g t", t=3)[
                        :, :, 1:2]))(e2),
                    inc=esem)
                m_c3 = E(eng, (lambda e2, ubf=ubf, t0b=t0b: lambda e:
                    e.tensor_copy(
                    ubf[e2][:].rearrange("p (g f) -> p g f", f=4)[
                        :, :, 3:4],
                    t0b[e2][:].rearrange("p (g t) -> p g t", t=3)[
                        :, :, 2:3]))(e2),
                    inc=esem)
                m_t1 = E(eng, (lambda e2, t1b=t1b, qwb=qwb, hc=hc:
                               lambda e:
                               e.tensor_scalar(
                    t1b[e2],
                    qwb[hc % 3][:].rearrange("p (g t) -> p g t", t=3)[
                        :, :, 0:1],
                    6, 63, ALU.logical_shift_right,
                    ALU.bitwise_and))(e2), waits=[(esem, m_c3)],
                    inc=esem)
                m_unp[hc] = E(eng, (lambda e2, ubf=ubf, t1b=t1b:
                    lambda e: e.tensor_copy(
                    ubf[e2][:].rearrange("p (g f) -> p g f", f=4)[
                        :, :, 1:2],
                    t1b[e2].rearrange("p (g o) -> p g o", o=1)))(e2),
                    waits=[(esem, m_t1)], inc=esem)
                if hc + 3 < 128:
                    qw_dma(hc + 3, [(esem, m_unp[hc])])
                wl = [(esem, m_unp[hc])]
                if qtr == 0 and (nti - 2) in m_wtr_nti:
                    wl.append(("PE", m_wtr_nti[nti - 2]))
                m_deq_by_hc[hc] = E("scalar", (lambda nti, qtr, wbf=wbf,
                                               ubf=ubf, e2=e2: lambda e:
                                               e.activation(
                    wbf[nti % 2][:, qtr * 1024:(qtr + 1) * 1024],
                    ubf[e2][:],
                    AF.Identity, bias=nzs[:, nti:nti + 1],
                    scale=fsc[:, nti:nti + 1]))(nti, qtr), waits=wl,
                    inc="AC")
                m_deqq[(grp, sub, qtr)] = m_deq_by_hc[hc]
    emit_unpack(0)
    for grp in range(16):
        gb = grp % 2
        nti0, nti1 = grp * 2, grp * 2 + 1
        for bt in range(16):
            qq = bt // 4
            wl = [("AC", m_deqq[(grp, 0, qq)]), ("AC", m_deqq[(grp, 1, qq)])]
            if wtb - 2 in m_wtev:
                wl.append(("DV", m_wtev[wtb - 2]))
            else:
                wl.append(("DV", m_l2dones[1]))
            if bt == 0 and grp - 2 in m_ffn_done:
                wl.append(("PE", m_ffn_done[grp - 2]))
            mT = None
            for i in range(4):
                kc = bt * 2 + i // 2
                nti = nti0 if i % 2 == 0 else nti1
                mT = E("tensor", (lambda kc, wtb, i, nti, wbf=wbf:
                                  lambda e:
                                  e.matmul(
                    psT[wtb % 2][:, i * 128:(i + 1) * 128],
                    wbf[nti % 2][:, kc * 128:(kc + 1) * 128], idb[:],
                    start=True, stop=True))(kc, wtb, i, nti),
                    waits=wl if i == 0 else (),
                    inc="PE" if i == 3 else None)
            m_wtr_nti[nti0] = mT
            m_wtr_nti[nti1] = mT

            def mk_wtev(gb, bt, wtb, wTb=wTb):
                def f(e):
                    return e.tensor_copy(
                        wTb[gb][:, bt * 512:(bt + 1) * 512],
                        psT[wtb % 2][:])
                return f
            m_wtev[wtb] = E("vector", mk_wtev(gb, bt, wtb),
                            waits=[("PE", mT)], inc="DV")
            wtb += 1
        if grp + 1 < 16:
            emit_unpack(grp + 1)
        psF = [psA[:, t * 256:(t + 1) * 256] for t in range(8)]
        m_bias = None
        for t in range(8):
            for kc in range(32):
                wl = []
                if t == 0 and kc == 0:
                    wl.append(("DV", m_wtev[wtb - 1]))
                    if grp - 1 in m_fev:
                        wl.append(("DV", m_fev[grp - 1]))
                    if grp == 0:
                        wl.append(("DV", m_consts))
                        wl.append(("WC", m_fb))
                E("tensor", (lambda t, kc, gb: lambda e: e.matmul(
                    psF[t],
                    ln2Ts[t // 4][:, kc * 512 + (t % 4) * 128:
                                  kc * 512 + (t % 4) * 128 + 128],
                    wTb[gb][:, kc * 256:(kc + 1) * 256],
                    start=(kc == 0), stop=False))(t, kc, gb),
                    waits=wl, inc=None)
            m_bias = E("tensor", (lambda t, grp: lambda e: e.matmul(
                psF[t],
                ones1[32 * (grp // 6):32 * (grp // 6) + 1, :],
                fbias[32 * (grp // 6):32 * (grp // 6) + 1,
                      (grp % 6) * 256:(grp % 6) * 256 + 256],
                start=False, stop=True))(t, grp),
                inc="PE" if t == 7 else None)
        m_ffn_done[grp] = m_bias

        def mk_fev(t, grp):
            if t < 4:
                src = x2rel[:, t * 256:(t + 1) * 256]
            else:
                src = x2s[:, (t - 4) * D + grp * 256:
                          (t - 4) * D + grp * 256 + 256]

            def f(e):
                return e.tensor_tensor(
                    outsb[:, (t % 4) * 256:(t % 4) * 256 + 256], psF[t],
                    src, ALU.add)
            return f

        def mk_st(t, grp):
            def f(e):
                return e.dma_start(
                    out_d[t * 128:(t + 1) * 128,
                          grp * 256:(grp + 1) * 256],
                    outsb[:, (t % 4) * 256:(t % 4) * 256 + 256])
            return f
        mm = None
        for t in range(4):
            wl = []
            if t == 0:
                wl = [("PE", m_bias), ("X0", m_x2r[grp])]
                if grp - 1 in m_store:
                    wl.append(("S0", m_store[grp - 1]))
            mm = E("vector", mk_fev(t, grp), waits=wl,
                   inc="DV" if t == 3 else None)
        m_fevA = mm
        if grp + 1 < 16:
            x2r_dma(grp + 1, [("DV", m_fevA)])
        ms = None
        for t in range(4):
            ms = E("sync", mk_st(t, grp),
                   waits=[("DV", m_fevA)] if t == 0 else (), inc="S0")
        m_storeA = ms
        mm = None
        for t in range(4, 8):
            wl = [("S0", m_storeA)] if t == 4 else []
            mm = E("vector", mk_fev(t, grp), waits=wl,
                   inc="DV" if t == 7 else None)
        m_fev[grp] = mm
        ms = None
        for t in range(4, 8):
            ms = E("sync", mk_st(t, grp),
                   waits=[("DV", m_fev[grp])] if t == 4 else (), inc="S0")
        m_store[grp] = ms

    bar()
    sclose("x2rel0", "outsb", "wTb1", "wTb0", "wbf1", "wbf0",
           "ubf1", "ubf0", "t0b1", "t0b0",
           "qwb2", "qwb1", "qwb0")

    for s in DMA_SEMS:
        tot = cnt[s]
        if tot > 0:
            E("sync", (lambda s, tot: lambda e: e.wait_ge(sems[s], tot))(
                s, tot))

    with nc.Block() as block:
        def mk_section(name):
            def section(eng):
                for waits, fn, inc in q[name]:
                    for s, v in waits:
                        eng.wait_ge(sems[s], v)
                    inst = fn(eng)
                    if inc is not None:
                        inst.then_inc(sems[inc], SEM_AMT[inc])
            return section

        block.sync(mk_section("sync"))
        block.gpsimd(mk_section("gpsimd"))
        block.vector(mk_section("vector"))
        block.scalar(mk_section("scalar"))
        block.tensor(mk_section("tensor"))

    for g in reversed(list(guards.values())):
        g.__exit__(None, None, None)
    psB_g.__exit__(None, None, None)
    psA_g.__exit__(None, None, None)
    for g in sem_guards:
        g.__exit__(None, None, None)
    return nc


def _alibi_slopes(n):
    start = 2.0 ** (-2.0 ** (-(np.log2(n) - 3)))
    return np.array([start ** (i + 1) for i in range(n)], dtype=np.float32)


def _host_tables(half):
    ql = QL[half]
    comp = QL[1 - half]
    B = []
    for j in range(NSLOT):
        B.append(comp[j])
        B.append(ql[j])
    slopes = _alibi_slopes(H)
    atabv = np.zeros((H * NSLOT, 2048), dtype=np.float32)
    posB = np.array(B)
    for j in range(NSLOT):
        K = KCOLS[j]
        e_j = EJ[j]
        q0 = ql[j] * 128
        col = np.arange(K)
        kb = col // 128
        posk = posB[kb] * 128 + col % 128
        valid = (kb < e_j) & (posB[kb] <= ql[j])
        base = (posk - q0).astype(np.float32)
        for h in range(H):
            atabv[h * NSLOT + j, 0:K] = np.where(valid, slopes[h] * base,
                                                 -60000.0)
    return B, atabv.astype(np.float16)


def kernel(L0_attn_hidden_states, L0_attn_Wq, L0_attn_Wkv, L0_attn_Wo,
           L0_ffn_qweight, L0_ffn_weight_scale, L0_ffn_weight_zero_point,
           L0_ffn_bias):
    x = np.ascontiguousarray(np.asarray(L0_attn_hidden_states,
                                        dtype=np.float32))
    Wq = np.ascontiguousarray(np.asarray(L0_attn_Wq, dtype=np.float32))
    Wkv = np.ascontiguousarray(np.asarray(L0_attn_Wkv, dtype=np.float32))
    Wo = np.ascontiguousarray(np.asarray(L0_attn_Wo, dtype=np.float32))
    qwt = np.ascontiguousarray(np.asarray(L0_ffn_qweight, dtype=np.int32))
    fsc = np.ascontiguousarray(
        np.asarray(L0_ffn_weight_scale, dtype=np.float32).reshape(32, 128).T)
    fzp = np.ascontiguousarray(
        np.asarray(L0_ffn_weight_zero_point,
                   dtype=np.float32).reshape(32, 128).T)
    fb3 = np.zeros((3, 1536), dtype=np.float32)
    fb3.reshape(-1)[:D] = np.asarray(L0_ffn_bias, dtype=np.float32).reshape(-1)
    fbias = fb3

    tri = np.where(np.arange(128)[None, :] > np.arange(128)[:, None],
                   np.float32(NEG), np.float32(0.0)).astype(np.float32)
    idf = np.eye(128, dtype=np.float32)
    ones1 = np.ones((65, 128), dtype=np.float16)

    in_maps = []
    perms = []
    for c in range(8):
        b, half = c // 2, c % 2
        B, atabv = _host_tables(half)
        xbc = np.ascontiguousarray(
            x[b].reshape(NBLK, 128, D)[B].reshape(S, D))
        in_maps.append(dict(
            xb=xbc, wq=Wq, wkv=Wkv, wo=Wo, qw=qwt, fsc=fsc, fzp=fzp,
            fbias=fbias, atab=atabv, tri=tri, idf=idf, ones1=ones1))
        perms.append(QL[half])

    nc = build_program()
    res = run_bass_kernel_spmd(nc, in_maps, core_ids=list(range(8)))

    y = np.empty((BATCH, S, D), dtype=np.float32)
    for c in range(8):
        b = c // 2
        o = np.asarray(res.results[c]["out"]).reshape(NSLOT, 128, D)
        yb = y[b].reshape(NBLK, 128, D)
        for j, blk in enumerate(perms[c]):
            yb[blk] = o[j]
    return y


# version 23 (final)
# speedup vs baseline: 1.2487x; 1.0655x over previous
import numpy as np
import ml_dtypes
import concourse.bass as bass
import concourse.mybir as mybir
from concourse.bass_utils import run_bass_kernel_spmd

F32 = mybir.dt.float32
BF16 = mybir.dt.bfloat16
F16 = mybir.dt.float16
I32 = mybir.dt.int32
AF = mybir.ActivationFunctionType
ALU = mybir.AluOpType
AX = mybir.AxisListType
bfd = ml_dtypes.bfloat16

D = 4096
S = 2048
BATCH = 4
H = 32
NBLK = 16
NSLOT = 8
RQ = 1024
EPS = 1e-5
QSCALE = 128.0 ** -0.5
NEG = -1e30

EJ = [2 * (j + 1) for j in range(NSLOT)]
SJ = [(e + 3) // 4 for e in EJ]
KCOLS = [s * 512 for s in SJ]

QL = {0: [0, 3, 4, 7, 8, 11, 12, 15], 1: [1, 2, 5, 6, 9, 10, 13, 14]}

ENGS = ("sync", "gpsimd", "vector", "scalar", "tensor")
DMA_SEMS = ("QC", "X0", "X1", "XR", "A0", "A1", "A2", "A3", "A4", "A5",
            "W0", "W1",
            "WC", "QW0", "QW1", "QW2", "QW3", "S0", "S1", "TP")
SEM_AMT = {**{s: 16 for s in DMA_SEMS}, "PE": 1, "DV": 1, "AC": 1, "PO": 1}


def build_program(debug=False, race_check=True):
    nc = bass.Bass("TRN2", target_bir_lowering=False,
                   detect_race_conditions=race_check)

    xb = nc.dram_tensor("xb", [S, D], F32, kind="ExternalInput").ap()
    wq = nc.dram_tensor("wq", [D, D], F32, kind="ExternalInput").ap()
    wkv = nc.dram_tensor("wkv", [D, 256], F32, kind="ExternalInput").ap()
    wo = nc.dram_tensor("wo", [D, D], F32, kind="ExternalInput").ap()
    qw = nc.dram_tensor("qw", [D, 3072], I32, kind="ExternalInput").ap()
    fsc_d = nc.dram_tensor("fsc", [128, 32], F32, kind="ExternalInput").ap()
    fzp_d = nc.dram_tensor("fzp", [128, 32], F32, kind="ExternalInput").ap()
    fbias_d = nc.dram_tensor("fbias", [3, 1536], F32,
                             kind="ExternalInput").ap()
    atab = nc.dram_tensor("atab", [H * NSLOT, 2048], F16,
                          kind="ExternalInput").ap()
    tri_d = nc.dram_tensor("tri", [128, 128], F16,
                           kind="ExternalInput").ap()
    idf_d = nc.dram_tensor("idf", [128, 128], F32, kind="ExternalInput").ap()
    ones_d = nc.dram_tensor("ones1", [65, 128], F16,
                            kind="ExternalInput").ap()
    out_d = nc.dram_tensor("out", [RQ, D], F32, kind="ExternalOutput").ap()

    q = {e: [] for e in ENGS}
    cnt = {s: 0 for s in SEM_AMT}
    last_wait = {e: {s: 0 for s in SEM_AMT} for e in ENGS}
    pending = {}

    def E(eng, fn, waits=(), inc=None):
        w = list(pending.pop(eng, {}).items()) + list(waits)
        lw = last_wait[eng]
        eff = []
        for s, v in w:
            if v is not None and v > lw[s]:
                lw[s] = v
                eff.append((s, v))
        m = None
        if inc is not None:
            cnt[inc] += SEM_AMT[inc]
            m = cnt[inc]
        q[eng].append((tuple(eff), fn, inc))
        return m

    def bar():
        snap = dict(cnt)
        for e in ENGS:
            p = pending.setdefault(e, {})
            for s, v in snap.items():
                if v > p.get(s, 0):
                    p[s] = v

    dbg_tensors = {}

    def tap(name, sb_t, shape, dt, waits=()):
        if not debug:
            return None
        d = nc.dram_tensor("dbg_" + name, shape, F32,
                           kind="ExternalOutput").ap()
        return E("gpsimd", (lambda d, sb_t: lambda e: e.dma_start(
            d, sb_t[:]))(d, sb_t), waits=waits, inc="TP")

    guards = {}

    uid = [0]

    def sopen(name, shape, dt, side="left"):
        uid[0] += 1
        g = nc.sbuf_tensor(f"{name}_u{uid[0]}", shape, dt, side=side)
        guards[name] = g
        return g.__enter__()

    def sclose(*names):
        for n in names:
            guards.pop(n).__exit__(None, None, None)

    idf = sopen("idf_sb", [128, 128], F32)
    idb = sopen("idb_sb", [128, 128], F16)
    ones1 = sopen("ones_sb", [65, 128], F16)
    fsc = sopen("fsc_sb", [128, 32], F32)
    fzp = sopen("fzp_sb", [128, 32], F32)
    nzs = fzp
    fbias = sopen("fbias_bf", [65, 1536], F16)
    epsb = sopen("epsb", [128, 1], F32)
    kT = sopen("kT", [128, 2048], F16)
    vT = sopen("vT", [128, NBLK * 128], F16)
    tri = sopen("tri_sb", [128, 128], F16)

    psA_g = nc.psum_tensor("psA", [128, 2048], F32)
    psA = psA_g.__enter__()
    psB_g = nc.psum_tensor("psB", [128, 2048], F32)
    psB = psB_g.__enter__()

    sems = {}
    sem_guards = []
    for s in SEM_AMT:
        g = nc.semaphore("sem" + s)
        sems[s] = g.__enter__()
        sem_guards.append(g)

    E("sync", lambda e: e.dma_start(tri[:], tri_d), inc="QC")
    E("sync", lambda e: e.dma_start(idf[:], idf_d), inc="QC")
    E("sync", lambda e: e.dma_start(ones1[:], ones_d), inc="QC")
    E("sync", lambda e: e.dma_start(fsc[:], fsc_d), inc="QC")
    m_cl = E("sync", lambda e: e.dma_start(fzp[:], fzp_d), inc="QC")
    E("vector", lambda e: e.tensor_copy(idb[:], idf[:]), waits=[("QC", m_cl)])
    E("vector", lambda e: e.memset(epsb[:], EPS))
    m_consts = E("vector", lambda e: e.scalar_tensor_tensor(
        fzp[:], fzp[:], -1.0, fsc[:], ALU.mult, ALU.mult), inc="DV")

    ln1T = sopen("ln1T", [128, 32 * 1024], F16, side="right")
    wkv_bf = sopen("wkv_bf", [128, 32 * 256], F16, side="right")
    xrt = [sopen(f"xrt{i}", [128, D], F32) for i in range(2)]
    lnbf = [sopen(f"lnbf{i}", [128, D], F16) for i in range(2)]
    scr_s = sopen("scr_s", [128, D], F32)
    scr_q = sopen("scr_q", [128, D], F32)
    roll = sopen("roll", [128, 32 * 128], F16)
    kvbf = [sopen(f"kvbf{i}", [128, 256], F16) for i in range(2)]
    st = {n: sopen("st_" + n, [128, NBLK], F32)
          for n in ("sum", "ssq", "mean", "msq", "var", "lnv", "rstd", "nmr")}

    for r3 in range(3):
        m_fb = E("gpsimd", (lambda r3: lambda e: e.dma_start(
            fbias[32 * r3:32 * r3 + 1, :], fbias_d[r3:r3 + 1, :]))(r3),
            inc="WC")
    m_wkv = E("gpsimd", lambda e: e.dma_start(
        wkv_bf[:].rearrange("p (s c) -> p s c", s=32),
        wkv.rearrange("(s p) c -> p s c", p=128)), inc="WC")

    psKV = [psA[:, 0:256], psA[:, 512:768]]
    psT = [psB[:, 0:512], psB[:, 512:1024]]
    psKT = [psB[:, 1024:1152], psB[:, 1536:1664]]

    m_x = {}
    m_apply = {}
    prev_m1 = [None, None]
    m_tev = {}
    m_kv = {}
    m_kvev = {}
    m_ktev = {}
    tb = 0

    def xdma(rt, wl):
        m_x[rt] = E("sync", (lambda r: lambda e: e.dma_start(
            xrt[r % 2][:], xb[r * 128:(r + 1) * 128, :]))(rt),
            waits=wl, inc="X%d" % (rt % 2))

    xdma(0, [])
    xdma(1, [])

    def emit_stats(rt):
        nonlocal_ = prev_m1
        r2 = rt % 2
        wl0 = [("X%d" % (rt % 2), m_x[rt])]
        if prev_m1[0] is not None:
            wl0.append(("DV", prev_m1[1]))
        m1 = E("scalar", (lambda r, r2: lambda e: e.activation(
            scr_s[:], xrt[r2][:], AF.Identity, bias=0.0, scale=1.0,
            accum_out=st["sum"][:, r:r + 1]))(rt, r2),
            waits=wl0, inc="AC")
        m2 = E("vector", (lambda r, r2: lambda e: e.scalar_tensor_tensor(
            scr_q[:], xrt[r2][:], 1.0, xrt[r2][:], ALU.mult, ALU.mult,
            accum_out=st["ssq"][:, r:r + 1]))(rt, r2), waits=wl0, inc="DV")
        prev_m1[0], prev_m1[1] = m1, m2
        m3 = E("vector", (lambda r: lambda e: e.tensor_scalar(
            st["mean"][:, r:r + 1], st["sum"][:, r:r + 1], 1.0 / D, None,
            ALU.mult))(rt), waits=[("AC", m1)], inc="DV")
        m4 = E("vector", (lambda r: lambda e: e.tensor_tensor(
            st["msq"][:, r:r + 1], st["mean"][:, r:r + 1],
            st["mean"][:, r:r + 1], ALU.mult))(rt), waits=[("DV", m3)],
            inc="DV")
        m_var = E("vector", (lambda r: lambda e: e.scalar_tensor_tensor(
            st["var"][:, r:r + 1], st["ssq"][:, r:r + 1], 1.0 / D,
            st["msq"][:, r:r + 1], ALU.mult, ALU.subtract))(rt),
            waits=[("DV", m2), ("DV", m4)], inc="DV")
        m_lnv = E("scalar", (lambda r: lambda e: e.activation(
            st["lnv"][:, r:r + 1], st["var"][:, r:r + 1], AF.Ln, bias=epsb[:],
            scale=1.0))(rt), waits=[("DV", m_var)], inc="AC")
        m_rstd = E("scalar", (lambda r: lambda e: e.activation(
            st["rstd"][:, r:r + 1], st["lnv"][:, r:r + 1], AF.Exp, bias=0.0,
            scale=-0.5))(rt), waits=[("AC", m_lnv)], inc="AC")
        m_nmr = E("vector", (lambda r: lambda e: e.scalar_tensor_tensor(
            st["nmr"][:, r:r + 1], st["mean"][:, r:r + 1], -1.0,
            st["rstd"][:, r:r + 1], ALU.mult, ALU.mult))(rt),
            waits=[("AC", m_rstd)], inc="DV")
        wl = [("DV", m_nmr)]
        if rt - 2 in m_kv:
            wl.append(("PE", m_kv[rt - 2]))
        m_apply[rt] = E("scalar", (lambda r, r2: lambda e: e.activation(
            lnbf[r2][:], xrt[r2][:], AF.Identity, bias=st["nmr"][:, r:r + 1],
            scale=st["rstd"][:, r:r + 1]))(rt, r2), waits=wl, inc="AC")

        if rt + 2 < NBLK:
            xdma(rt + 2, [("AC", m_apply[rt])])

    emit_stats(0)
    for rt in range(NBLK):
        r2 = rt % 2
        if rt + 1 < NBLK:
            emit_stats(rt + 1)
        odd = (rt % 2 == 1)
        j = rt // 2
        rt_tb0 = tb
        for bt in range(8):
            wl = [("AC", m_apply[rt])]
            if tb - 2 in m_tev:
                wl.append(("DV", m_tev[tb - 2]))
            m = None
            for i in range(4):
                kc = bt * 4 + i
                m = E("tensor", (lambda r2, kc, t2, i: lambda e: e.matmul(
                    psT[t2][:, i * 128:(i + 1) * 128],
                    lnbf[r2][:, kc * 128:(kc + 1) * 128], idb[:],
                    start=True, stop=True))(r2, kc, tb % 2, i),
                    waits=wl if i == 0 else (), inc="PE" if i == 3 else None)
            if odd:
                def mk_ev(jj, bt, t2):
                    def f(e):
                        d = ln1T[:].rearrange("p (kc r) -> p kc r", r=1024)
                        d = d[:, bt * 4:(bt + 1) * 4,
                              jj * 128:jj * 128 + 128]
                        return e.tensor_copy(d, psT[t2][:].rearrange(
                            "p (k c) -> p k c", k=4))
                    return f
                m_tev[tb] = E("vector", mk_ev(j, bt, tb % 2),
                              waits=[("PE", m)], inc="DV")
            else:
                m_tev[tb] = E("vector", (lambda bt, t2: lambda e:
                                         e.tensor_copy(
                    roll[:, bt * 512:(bt + 1) * 512], psT[t2][:]))(bt, tb % 2),
                    waits=[("PE", m)], inc="DV")
            tb += 1

        if odd:
            src = (lambda jj: lambda kc: ln1T[
                :, kc * 1024 + jj * 128: kc * 1024 + jj * 128 + 128])(j)
        else:
            src = lambda kc: roll[:, kc * 128:(kc + 1) * 128]
        mkv = None
        for kc in range(32):
            wl = []
            if kc % 4 == 0:
                wl.append(("DV", m_tev[rt_tb0 + kc // 4]))
            if kc == 0:
                wl.append(("WC", m_wkv))
                if rt - 2 in m_kvev:
                    wl.append(("DV", m_kvev[rt - 2]))
            mkv = E("tensor", (lambda kc, r2, src: lambda e: e.matmul(
                psKV[r2], src(kc), wkv_bf[:, kc * 256:(kc + 1) * 256],
                start=(kc == 0), stop=(kc == 31)))(kc, r2, src),
                waits=wl, inc="PE" if kc == 31 else None)
        m_kv[rt] = mkv
        m_kvev[rt] = E("vector", (lambda r2: lambda e: e.tensor_copy(
            kvbf[r2][:], psKV[r2]))(r2), waits=[("PE", m_kv[rt])], inc="DV")
        E("vector", (lambda r, r2: lambda e: e.tensor_copy(
            vT[:, r * 128:(r + 1) * 128], kvbf[r2][:, 128:256]))(rt, r2),
            waits=[("DV", m_kvev[rt])])
        wl = [("DV", m_kvev[rt])]
        if rt - 2 in m_ktev:
            wl.append(("DV", m_ktev[rt - 2]))
        m_kt = E("tensor", (lambda r2: lambda e: e.matmul(
            psKT[r2], kvbf[r2][:, 0:128], idb[:], start=True,
            stop=True))(r2), waits=wl, inc="PE")
        m_ktev[rt] = E("vector", (lambda r, r2: lambda e: e.tensor_copy(
            kT[:, r * 128:(r + 1) * 128], psKT[r2]))(rt, r2),
            waits=[("PE", m_kt)], inc="DV")

    bar()
    tap("kT", kT, [128, 2048], F16)
    tap("vT", vT, [128, NBLK * 128], F16)
    tap("ln1T", ln1T, [128, 32 * 1024], F16)
    bar()
    sclose(*("st_" + n for n in reversed(("sum", "ssq", "mean", "msq", "var",
                                          "lnv", "rstd", "nmr"))))
    sclose("kvbf1", "kvbf0", "roll", "scr_q", "scr_s", "lnbf1", "lnbf0",
           "xrt1", "xrt0")
    sclose("wkv_bf")

    qT = sopen("qT", [128, 32 * 1024], F16)
    wqb = [sopen(f"wqb{i}", [128, 8 * 512], F16) for i in range(2)]


    psWq = [psA[:, t * 512:(t + 1) * 512] for t in range(4)] + \
           [psB[:, t * 512:(t + 1) * 512] for t in range(4)]

    m_wq_done = {}
    m_wqc = {}
    m_wqev = {}

    def wq_dma(ci, wl):
        ntg, kcs = ci // 4, ci % 4
        m_wqc[ci] = E("gpsimd", (lambda ntg, kcs, ci: lambda e: e.dma_start(
            wqb[ci % 2][:].rearrange("p (s c) -> p s c", s=8),
            wq[kcs * 1024:(kcs + 1) * 1024,
               ntg * 512:(ntg + 1) * 512].rearrange(
                "(s p) c -> p s c", p=128)))(ntg, kcs, ci),
            waits=wl, inc="W%d" % (ci % 2))

    wq_dma(0, [])
    wq_dma(1, [])
    for ntg in range(8):
        for kcs in range(4):
            ci = ntg * 4 + kcs
            mlast = None
            for kc8 in range(8):
                kc = kcs * 8 + kc8
                for i in range(4):
                    for rs in range(2):
                        wl = []
                        if kc8 == 0 and i == 0 and rs == 0:
                            wl.append(("W%d" % (ci % 2), m_wqc[ci]))
                            if kc == 0 and ntg - 1 in m_wqev:
                                wl.append(("DV", m_wqev[ntg - 1]))
                        mlast = E("tensor", (lambda ci, kc8, i, rs, kc:
                                             lambda e: e.matmul(
                            psWq[i * 2 + rs],
                            wqb[ci % 2][:, kc8 * 512 + i * 128:
                                        kc8 * 512 + i * 128 + 128],
                            ln1T[:, kc * 1024 + rs * 512:
                                 kc * 1024 + rs * 512 + 512],
                            start=(kc == 0),
                            stop=(kc == 31)))(ci, kc8, i, rs, kc),
                            waits=wl,
                            inc="PE" if (kc8 == 7 and i == 3 and rs == 1)
                            else None)
            m_wq_done[ci] = mlast
            if ci + 2 < 32:
                wq_dma(ci + 2, [("PE", m_wq_done[ci])])
        for i in range(4):
            for rs in range(2):
                h = ntg * 4 + i
                mm = E("vector", (lambda h, rs, i: lambda e: e.tensor_scalar(
                    qT[:, h * 1024 + rs * 512: h * 1024 + rs * 512 + 512],
                    psWq[i * 2 + rs], QSCALE, None, ALU.mult))(h, rs, i),
                    waits=[("PE", m_wq_done[ntg * 4 + 3])]
                    if (i == 0 and rs == 0) else (),
                    inc="DV" if (i == 3 and rs == 1) else None)
        m_wqev[ntg] = mm

    bar()
    tap("qT", qT, [128, 32 * 1024], F16)
    bar()
    sclose("wqb1", "wqb0")
    sclose("ln1T")

    AVT_B = sopen("AVT_B", [128, 32 * 512], F16, side="right")
    AVT_A = sopen("AVT_A", [128, 32 * 512], F16, side="right")
    attn_sb = [sopen(f"attn_sb{i}", [128, 4 * 2048], F16) for i in range(2)]
    diagm = [sopen(f"diagm{i}", [128, 4 * 128], F16) for i in range(2)]
    atst = [sopen(f"atst{i}", [1, 2048], F16) for i in range(6)]
    attnT = [sopen(f"attnT{i}", [128, 512], F16) for i in range(2)]
    lbuf = sopen("lbuf", [128, 4], F32)
    linv = sopen("linv", [128, 4], F32)
    negmax = sopen("negmax", [128, 2], F32)
    pmax = sopen("pmax", [128, 16], F32)

    psT2 = [psB[:, 0:512], psB[:, 512:1024]]
    psAVT = [psB[:, 1024:1536], psB[:, 1536:2048]]

    m_ali = {}
    m_rank1 = {}
    m_exp = {}
    m_Btr = {}
    m_Tev2 = {}
    m_AVTev = {}
    m_diag = {}
    bstate = {}
    a_ctr = 0

    def prev2B(j, g):
        if g >= 2:
            return (j, g - 2)
        if j >= 1:
            return (j - 1, g + 6)
        return None

    def emit_B_chunk(bj, bg, kbs):
        e_b = EJ[bj]
        st = bstate.setdefault((bj, bg), {})

        def emit_av(kb):
            wl = [("DV", m_Tev2[(bj, bg, kb)])]
            if kb == 0:
                pk = prev2B(bj, bg)
                if pk in m_AVTev:
                    wl.append(("DV", m_AVTev[pk]))
            st["m_av"] = E("tensor", (lambda kb, bg, e_b: lambda e: e.matmul(
                psAVT[bg % 2], vT[:, kb * 128:(kb + 1) * 128],
                attnT[kb % 2][:],
                start=(kb == 0), stop=(kb == e_b - 1)))(kb, bg, e_b),
                waits=wl, inc="PE" if kb == e_b - 1 else None)

        for kb in kbs:
            wl = []
            if (bj, bg, kb - 2) in m_Tev2:
                wl.append(("DV", m_Tev2[(bj, bg, kb - 2)]))
            if kb == 0:
                wl.append(("AC", m_diag[(bj, bg)]))
            mT = None
            for hh in range(4):
                mT = E("tensor", (lambda kb, hh, bg: lambda e: e.matmul(
                    psT2[kb % 2][:, hh * 128:(hh + 1) * 128],
                    attn_sb[bg % 2][:, hh * 2048 + kb * 128:
                                    hh * 2048 + kb * 128 + 128],
                    diagm[bg % 2][:, hh * 128:(hh + 1) * 128],
                    start=True, stop=True))(kb, hh, bg),
                    waits=wl if hh == 0 else (),
                    inc="PE" if hh == 3 else None)
            m_Tev2[(bj, bg, kb)] = E("vector", (lambda kb: lambda e:
                                               e.tensor_copy(
                attnT[kb % 2][:], psT2[kb % 2][:]))(kb),
                waits=[("PE", mT)], inc="DV")
            if "pend" in st:
                emit_av(st.pop("pend"))
            st["pend"] = kb
        if kbs and kbs[-1] == e_b - 1:
            emit_av(st.pop("pend"))
            m_Btr[(bj, bg)] = st["m_av"]
            AVT = AVT_A if bj < 4 else AVT_B
            jw = bj % 4

            def mk_avtev(AVT, bg, jw):
                def f(e):
                    d = AVT[:].rearrange("p (h r) -> p h r", r=512)
                    d = d[:, bg * 4:(bg + 1) * 4, jw * 128: jw * 128 + 128]
                    return e.tensor_copy(d, psAVT[bg % 2].rearrange(
                        "p (h c) -> p h c", h=4))
                return f
            m_AVTev[(bj, bg)] = E("vector", mk_avtev(AVT, bg, jw),
                                  waits=[("PE", st["m_av"])], inc="DV")

    prevB = None
    for j in range(NSLOT):
        K = KCOLS[j]
        s = SJ[j]
        e_j = EJ[j]
        depth2 = K <= 1024
        for g in range(8):
            if prevB is not None:
                pe_b = EJ[prevB[0]]
                kbs_all = list(range(pe_b))
                chunks = [kbs_all[(c * pe_b) // 4:((c + 1) * pe_b) // 4]
                          for c in range(4)]
            else:
                chunks = [[], [], [], []]
            for hh in range(4):
                h = g * 4 + hh
                a = a_ctr
                a_ctr += 1
                wl = []
                if a - 6 >= 0:
                    wl.append(("PE", m_rank1[a - 6]))
                m_ali[a] = E("sync", (lambda j, h, a, K: lambda e:
                                      e.dma_start(
                    atst[a % 6][0:1, 0:K],
                    atab[h * NSLOT + j: h * NSLOT + j + 1, 0:K]))(
                        j, h, a, K),
                    waits=wl, inc="A%d" % (a % 6))

                if depth2:
                    roff = (hh % 2) * 1024
                    prev = (j, g, hh - 2) if hh >= 2 else                         ((j, g - 1, hh + 2) if g >= 1 else
                         ((j - 1, 7, hh + 2) if j >= 1 else None))
                else:
                    roff = 0
                    prev = (j, g, hh - 1) if hh >= 1 else                         ((j, g - 1, 3) if g >= 1 else
                         ((j - 1, 7, 3) if j >= 1 else None))
                wl = [("A%d" % (a % 6), m_ali[a])]
                if prev in m_exp:
                    wl.append(("AC", m_exp[prev]))
                mr = None
                for i in range(s):
                    mr = E("tensor", (lambda roff, i, a: lambda e: e.matmul(
                        psA[:, roff + i * 512: roff + (i + 1) * 512],
                        ones1[0:1, :],
                        atst[a % 6][0:1, i * 512:(i + 1) * 512],
                        start=True, stop=False))(roff, i, a),
                        waits=wl if i == 0 else (),
                        inc="PE" if i == s - 1 else None)
                m_rank1[a] = mr
                dslab = (2 * j + 1) // 4
                mqk = []
                for i in range(s):
                    mqk.append(E("tensor", (lambda roff, i, h, j, dslab:
                                            lambda e:
                                            e.matmul(
                        psA[:, roff + i * 512: roff + (i + 1) * 512],
                        qT[:, h * 1024 + j * 128: h * 1024 + j * 128 + 128],
                        kT[:, i * 512:(i + 1) * 512],
                        start=False,
                        stop=(i != dslab)))(roff, i, h, j, dslab),
                        inc=None if i == dslab else "PE"))
                    if i == dslab:
                        mqk[i] = E("tensor", (lambda roff, j: lambda e:
                                              e.matmul(
                            psA[:, roff + (2 * j + 1) * 128:
                                roff + (2 * j + 2) * 128],
                            tri[:], idb[:],
                            start=False, stop=True))(roff, j), inc="PE")
                m_pm = None
                for i in range(s):
                    wlp = [("PE", mqk[i])]
                    m_pm = E("vector", (lambda roff, i, hh: lambda e:
                                        e.reduce_max(
                        pmax[:, hh * 4 + i:hh * 4 + i + 1],
                        psA[:, roff + i * 512: roff + (i + 1) * 512],
                        AX.X, negate=True))(roff, i, hh),
                        waits=wlp, inc="DV")
                m_cmb = E("vector", (lambda hh, s: lambda e: e.tensor_reduce(
                    negmax[:, hh % 2:hh % 2 + 1],
                    pmax[:, hh * 4:hh * 4 + s], AX.X, ALU.min))(hh, s),
                    waits=[("DV", m_pm)], inc="DV")
                wl2 = [("DV", m_cmb)]
                pb = prev2B(j, g)
                if pb in m_Btr:
                    wl2.append(("PE", m_Btr[pb]))
                m_exp[(j, g, hh)] = E("scalar", (lambda roff, K, hh, g:
                                                 lambda e: e.activation(
                    attn_sb[g % 2][:, hh * 2048: hh * 2048 + K],
                    psA[:, roff:roff + K],
                    AF.Exp, bias=negmax[:, hh % 2:hh % 2 + 1], scale=1.0,
                    accum_out=lbuf[:, hh:hh + 1]))(roff, K, hh, g),
                    waits=wl2, inc="AC")
                if prevB is not None:
                    emit_B_chunk(prevB[0], prevB[1], chunks[hh])
            wl = [("AC", m_exp[(j, g, 3)])]
            m_rcp = E("vector", lambda e: e.reciprocal(linv[:], lbuf[:]),
                      waits=wl, inc="DV")
            pk = prev2B(j, g)
            md = None
            for hh in range(4):
                wl = [("DV", m_rcp)] if hh == 0 else ()
                if hh == 0 and pk in m_Btr:
                    wl.append(("PE", m_Btr[pk]))
                md = E("scalar", (lambda hh, g: lambda e: e.activation(
                    diagm[g % 2][:, hh * 128:(hh + 1) * 128], idf[:],
                    AF.Identity, bias=0.0,
                    scale=linv[:, hh:hh + 1]))(hh, g),
                    waits=wl, inc="AC" if hh == 3 else None)
            m_diag[(j, g)] = md
            prevB = (j, g)
    emit_B_chunk(prevB[0], prevB[1], list(range(EJ[prevB[0]])))

    bar()
    tap("AVT_A", AVT_A, [128, 32 * 512], F16)
    tap("AVT_B", AVT_B, [128, 32 * 512], F16)
    bar()
    sclose("pmax", "negmax", "linv", "lbuf", "attnT1", "attnT0",
           "atst5", "atst4", "atst3", "atst2", "atst1", "atst0",
           "diagm1", "diagm0",
           "attn_sb1", "attn_sb0", "qT",
           "tri_sb", "vT", "kT")

    x2s = sopen("x2s", [128, 4 * D], F32)
    ln2Ts = [sopen("ln2T0", [128, 32 * 512], F16)]
    x2sp = nc.dram_tensor("x2sp", [512, D], F32, kind="Internal").ap()
    m_l2dones = []
    m_spill = None

    psWo = [psA[:, t * 512:(t + 1) * 512] for t in range(4)] + \
           [psB[:, t * 512:(t + 1) * 512] for t in range(4)]

    for slab in range(2):
        AVT = AVT_A if slab == 0 else AVT_B
        if slab == 1:
            ln2Ts.append(sopen("ln2T1", [128, 32 * 512], F16))
        xres = [sopen("xres0", [128, D], F32)]
        xres = [xres[0], xres[0]]
        wob = [sopen(f"wob{i}", [128, 8 * 512], F16) for i in range(2)]

        m_woc = {}
        m_wo_done = {}
        m_woev = {}

        def wo_dma(ci, wl):
            nt, kcs = ci // 4, ci % 4
            m_woc[ci] = E("gpsimd", (lambda nt, kcs, ci, wob=wob: lambda e:
                                     e.dma_start(
                wob[ci % 2][:].rearrange("p (s c) -> p s c", s=8),
                wo[kcs * 1024:(kcs + 1) * 1024,
                   nt * 512:(nt + 1) * 512].rearrange(
                    "(s p) c -> p s c", p=128)))(nt, kcs, ci),
                waits=wl, inc="W%d" % (ci % 2))

        wo_dma(0, [])
        wo_dma(1, [])
        for nt in range(8):
            for kcs in range(4):
                ci = nt * 4 + kcs
                mlast = None
                for kc8 in range(8):
                    kc = kcs * 8 + kc8
                    for rtw in range(4):
                        wl = []
                        if kc8 == 0 and rtw == 0:
                            wl.append(("W%d" % (ci % 2), m_woc[ci]))
                            if kc == 0 and nt - 2 in m_woev:
                                wl.append(("DV", m_woev[nt - 2]))
                        mlast = E("tensor", (lambda ci, kc8, kc, rtw, nt,
                                             AVT=AVT, wob=wob:
                                             lambda e: e.matmul(
                            psWo[(nt % 2) * 4 + rtw],
                            AVT[:, kc * 512 + rtw * 128:
                                kc * 512 + rtw * 128 + 128],
                            wob[ci % 2][:, kc8 * 512:(kc8 + 1) * 512],
                            start=(kc == 0),
                            stop=(kc == 31)))(ci, kc8, kc, rtw, nt),
                            waits=wl,
                            inc="PE" if (kc8 == 7 and rtw == 3) else None)
                m_wo_done[ci] = mlast
                if ci + 2 < 32:
                    wo_dma(ci + 2, [("PE", m_wo_done[ci])])
            mm = None
            for rtw in range(4):
                wl = []
                if rtw == 0:
                    wl = [("PE", m_wo_done[nt * 4 + 3])]
                    if slab == 1 and nt == 0:
                        wl.append(("S1", m_spill))
                mm = E("vector", (lambda nt, rtw: lambda e: e.tensor_copy(
                    x2s[:, rtw * D + nt * 512: rtw * D + nt * 512 + 512],
                    psWo[(nt % 2) * 4 + rtw]))(nt, rtw),
                    waits=wl,
                    inc="DV" if rtw == 3 else None)
            m_woev[nt] = mm

        m_xr = {}
        m_res = {}

        def xr_dma(rtw, wl):
            j = slab * 4 + rtw
            m_xr[rtw] = E("sync", (lambda j, rtw, xres=xres: lambda e:
                e.dma_start(
                xres[rtw % 2][:],
                xb[(2 * j + 1) * 128:(2 * j + 2) * 128, :]))(j, rtw),
                waits=wl, inc="XR")

        sclose("wob1", "wob0")
        sclose("AVT_A" if slab == 0 else "AVT_B")
        xr_dma(0, [])

        lnbf2 = [sopen(f"lnbf2_{i}", [128, D], F16) for i in range(2)]
        scr2s = sopen("scr2s", [128, D], F32)
        scr2q = sopen("scr2q", [128, D], F32)
        s2 = {n: sopen("s2_" + n, [128, 4], F32)
              for n in ("sum", "ssq", "mean", "msq", "var", "lnv", "rstd",
                        "nmr")}

        m_l2ap = {}
        m_l2tev = {}
        m_l2kv = {}
        tb3 = 0
        for rtw in range(4):
            m_res[rtw] = E("vector", (lambda rtw, xres=xres: lambda e:
                e.tensor_tensor(
                x2s[:, rtw * D:(rtw + 1) * D], x2s[:, rtw * D:(rtw + 1) * D],
                xres[rtw % 2][:], ALU.add))(rtw),
                waits=[("XR", m_xr[rtw]), ("DV", m_woev[7])], inc="DV")
            if rtw + 1 < 4:
                xr_dma(rtw + 1, [("DV", m_res[rtw])])
            wl0 = [("DV", m_res[rtw])]
            m1 = E("scalar", (lambda rtw, scr2s=scr2s, s2=s2: lambda e:
                e.activation(
                scr2s[:], x2s[:, rtw * D:(rtw + 1) * D], AF.Identity,
                bias=0.0, scale=1.0,
                accum_out=s2["sum"][:, rtw:rtw + 1]))(rtw),
                waits=wl0, inc="AC")
            m2 = E("vector", (lambda rtw, scr2q=scr2q, s2=s2: lambda e:
                e.scalar_tensor_tensor(
                scr2q[:], x2s[:, rtw * D:(rtw + 1) * D], 1.0,
                x2s[:, rtw * D:(rtw + 1) * D], ALU.mult, ALU.mult,
                accum_out=s2["ssq"][:, rtw:rtw + 1]))(rtw), inc="DV")
            m3 = E("vector", (lambda rtw, s2=s2: lambda e: e.tensor_scalar(
                s2["mean"][:, rtw:rtw + 1], s2["sum"][:, rtw:rtw + 1],
                1.0 / D, None, ALU.mult))(rtw), waits=[("AC", m1)],
                inc="DV")
            m4 = E("vector", (lambda rtw, s2=s2: lambda e: e.tensor_tensor(
                s2["msq"][:, rtw:rtw + 1], s2["mean"][:, rtw:rtw + 1],
                s2["mean"][:, rtw:rtw + 1], ALU.mult))(rtw),
                waits=[("DV", m3)], inc="DV")
            m_var = E("vector", (lambda rtw, s2=s2: lambda e:
                e.scalar_tensor_tensor(
                s2["var"][:, rtw:rtw + 1], s2["ssq"][:, rtw:rtw + 1],
                1.0 / D, s2["msq"][:, rtw:rtw + 1], ALU.mult,
                ALU.subtract))(rtw), waits=[("DV", m2), ("DV", m4)],
                inc="DV")
            m_lnv = E("scalar", (lambda rtw, s2=s2: lambda e: e.activation(
                s2["lnv"][:, rtw:rtw + 1], s2["var"][:, rtw:rtw + 1], AF.Ln,
                bias=epsb[:], scale=1.0))(rtw), waits=[("DV", m_var)],
                inc="AC")
            m_rstd = E("scalar", (lambda rtw, s2=s2: lambda e: e.activation(
                s2["rstd"][:, rtw:rtw + 1], s2["lnv"][:, rtw:rtw + 1],
                AF.Exp, bias=0.0, scale=-0.5))(rtw), waits=[("AC", m_lnv)],
                inc="AC")
            m_nmr = E("vector", (lambda rtw, s2=s2: lambda e:
                                 e.scalar_tensor_tensor(
                s2["nmr"][:, rtw:rtw + 1], s2["mean"][:, rtw:rtw + 1], -1.0,
                s2["rstd"][:, rtw:rtw + 1], ALU.mult, ALU.mult))(rtw),
                waits=[("AC", m_rstd)], inc="DV")
            wl = [("DV", m_nmr)]
            if rtw - 2 in m_l2kv:
                wl.append(("PE", m_l2kv[rtw - 2]))
            m_l2ap[rtw] = E("scalar", (lambda rtw, lnbf2=lnbf2, s2=s2:
                lambda e: e.activation(
                lnbf2[rtw % 2][:], x2s[:, rtw * D:(rtw + 1) * D], AF.Identity,
                bias=s2["nmr"][:, rtw:rtw + 1],
                scale=s2["rstd"][:, rtw:rtw + 1]))(rtw), waits=wl, inc="AC")
            if slab == 0:
                m_spill = E("sync", (lambda rtw: lambda e: e.dma_start(
                    x2sp[rtw * 128:(rtw + 1) * 128, :],
                    x2s[:, rtw * D:(rtw + 1) * D]))(rtw),
                    waits=[("AC", m_l2ap[rtw])], inc="S1")
            mT = None
            for bt in range(8):
                wl = [("AC", m_l2ap[rtw])]
                if tb3 - 2 in m_l2tev:
                    wl.append(("DV", m_l2tev[tb3 - 2]))
                for i in range(4):
                    kc = bt * 4 + i
                    mT = E("tensor", (lambda rtw, kc, t3, i, lnbf2=lnbf2:
                                      lambda e:
                                      e.matmul(
                        psT[t3][:, i * 128:(i + 1) * 128],
                        lnbf2[rtw % 2][:, kc * 128:(kc + 1) * 128], idb[:],
                        start=True, stop=True))(rtw, kc, tb3 % 2, i),
                        waits=wl if i == 0 else (),
                        inc="PE" if i == 3 else None)

                def mk_l2ev(rtw, bt, t3, slab=slab):
                    def f(e):
                        d = ln2Ts[slab][:].rearrange(
                            "p (kc r) -> p kc r", r=512)
                        d = d[:, bt * 4:(bt + 1) * 4,
                              rtw * 128: rtw * 128 + 128]
                        return e.tensor_copy(d, psT[t3][:].rearrange(
                            "p (k c) -> p k c", k=4))
                    return f
                m_l2tev[tb3] = E("vector", mk_l2ev(rtw, bt, tb3 % 2),
                                 waits=[("PE", mT)], inc="DV")
                tb3 += 1
            m_l2kv[rtw] = mT
        m_l2dones.append(m_l2tev[tb3 - 1])
        if slab == 0:
            bar()
            tap("x2s", x2s, [128, 4 * D], F32)
            tap("ln2T", ln2Ts[0], [128, 32 * 512], F16)
            bar()

        sclose(*("s2_" + n for n in reversed(("sum", "ssq", "mean", "msq",
                                              "var", "lnv", "rstd", "nmr"))))
        sclose("scr2q", "scr2s", "lnbf2_1", "lnbf2_0", "xres0")

    qwb = [sopen(f"qwb{i}", [128, 768], I32) for i in range(3)]
    t0b = [sopen(f"t0b{i}", [128, 768], I32) for i in range(2)]
    t1b = [t0b[0][:, 512:768], t0b[1][:, 512:768]]
    ubf = [sopen(f"ubf{i}", [128, 1024], F16) for i in range(2)]
    wbf = [sopen(f"wbf{i}", [128, D], F16) for i in range(2)]
    wTb = [sopen(f"wTb{i}", [128, 32 * 256], F16) for i in range(2)]
    outsb = sopen("outsb", [128, 4 * 256], F32)
    x2rel = sopen("x2rel0", [128, 4 * 256], F32)

    m_qwc = {}
    m_unp = {}
    m_wtr_nti = {}
    m_ffn_done = {}
    m_fev = {}
    m_store = {}
    m_wtev = {}
    m_x2r = {}
    wtb = 0

    def qw_dma(hc, wl):
        nti, qtr = hc // 4, hc % 4
        m_qwc[hc] = E("sync", (lambda nti, qtr, hc, qwb=qwb: lambda e:
                               e.dma_start(
            qwb[hc % 3][:],
            qw[nti * 128:(nti + 1) * 128,
               qtr * 768:(qtr + 1) * 768]))(nti, qtr, hc),
            waits=wl, inc="QW%d" % (hc % 3))

    def x2r_dma(grp, wl):
        ms = None
        for t in range(4):
            ms = E("scalar", (lambda grp, t: lambda e: e.dma_start(
                x2rel[:, t * 256:(t + 1) * 256],
                x2sp[t * 128:(t + 1) * 128,
                     grp * 256:(grp + 1) * 256]))(grp, t),
                waits=wl if t == 0 else (), inc="X0")
        m_x2r[grp] = ms

    for i in range(3):
        qw_dma(i, [("DV", m_l2dones[1])])
    x2r_dma(0, [("S1", m_spill)])
    m_deq_by_hc = {}
    m_deqq = {}

    def emit_unpack(grp):
        for sub in range(2):
            nti = grp * 2 + sub
            for qtr in range(4):
                hc = nti * 4 + qtr
                eng = "vector"
                esem = "DV"
                e2 = hc % 2
                wl = [("QW%d" % (hc % 3), m_qwc[hc])]
                if hc - 2 in m_deq_by_hc:
                    wl.append(("AC", m_deq_by_hc[hc - 2]))
                if hc - 2 in m_unp:
                    wl.append((esem, m_unp[hc - 2]))
                m_t0 = E(eng, (lambda e2, t0b=t0b, qwb=qwb, hc=hc:
                               lambda e:
                               e.tensor_scalar(
                    t0b[e2][:], qwb[hc % 3][:], 63, None,
                    ALU.bitwise_and))(e2),
                    waits=wl, inc=esem)
                E(eng, (lambda e2, ubf=ubf, t0b=t0b: lambda e:
                    e.tensor_copy(
                    ubf[e2][:].rearrange("p (g f) -> p g f", f=4)[
                        :, :, 0:1],
                    t0b[e2][:].rearrange("p (g t) -> p g t", t=3)[
                        :, :, 0:1]))(e2),
                    waits=[(esem, m_t0)], inc=esem)
                E(eng, (lambda e2, ubf=ubf, t0b=t0b: lambda e:
                    e.tensor_copy(
                    ubf[e2][:].rearrange("p (g f) -> p g f", f=4)[
                        :, :, 2:3],
                    t0b[e2][:].rearrange("p (g t) -> p g t", t=3)[
                        :, :, 1:2]))(e2),
                    inc=esem)
                m_c3 = E(eng, (lambda e2, ubf=ubf, t0b=t0b: lambda e:
                    e.tensor_copy(
                    ubf[e2][:].rearrange("p (g f) -> p g f", f=4)[
                        :, :, 3:4],
                    t0b[e2][:].rearrange("p (g t) -> p g t", t=3)[
                        :, :, 2:3]))(e2),
                    inc=esem)
                m_t1 = E(eng, (lambda e2, t1b=t1b, qwb=qwb, hc=hc:
                               lambda e:
                               e.tensor_scalar(
                    t1b[e2],
                    qwb[hc % 3][:].rearrange("p (g t) -> p g t", t=3)[
                        :, :, 0:1],
                    6, 63, ALU.logical_shift_right,
                    ALU.bitwise_and))(e2), waits=[(esem, m_c3)],
                    inc=esem)
                m_unp[hc] = E(eng, (lambda e2, ubf=ubf, t1b=t1b:
                    lambda e: e.tensor_copy(
                    ubf[e2][:].rearrange("p (g f) -> p g f", f=4)[
                        :, :, 1:2],
                    t1b[e2].rearrange("p (g o) -> p g o", o=1)))(e2),
                    waits=[(esem, m_t1)], inc=esem)
                if hc + 3 < 128:
                    qw_dma(hc + 3, [(esem, m_unp[hc])])
                wl = [(esem, m_unp[hc])]
                if qtr == 0 and (nti - 2) in m_wtr_nti:
                    wl.append(("PE", m_wtr_nti[nti - 2]))
                m_deq_by_hc[hc] = E("scalar", (lambda nti, qtr, wbf=wbf,
                                               ubf=ubf, e2=e2: lambda e:
                                               e.activation(
                    wbf[nti % 2][:, qtr * 1024:(qtr + 1) * 1024],
                    ubf[e2][:],
                    AF.Identity, bias=nzs[:, nti:nti + 1],
                    scale=fsc[:, nti:nti + 1]))(nti, qtr), waits=wl,
                    inc="AC")
                m_deqq[(grp, sub, qtr)] = m_deq_by_hc[hc]
    emit_unpack(0)
    for grp in range(16):
        gb = grp % 2
        nti0, nti1 = grp * 2, grp * 2 + 1
        for bt in range(16):
            qq = bt // 4
            wl = [("AC", m_deqq[(grp, 0, qq)]), ("AC", m_deqq[(grp, 1, qq)])]
            if wtb - 2 in m_wtev:
                wl.append(("DV", m_wtev[wtb - 2]))
            else:
                wl.append(("DV", m_l2dones[1]))
            if bt == 0 and grp - 2 in m_ffn_done:
                wl.append(("PE", m_ffn_done[grp - 2]))
            mT = None
            for i in range(4):
                kc = bt * 2 + i // 2
                nti = nti0 if i % 2 == 0 else nti1
                mT = E("tensor", (lambda kc, wtb, i, nti, wbf=wbf:
                                  lambda e:
                                  e.matmul(
                    psT[wtb % 2][:, i * 128:(i + 1) * 128],
                    wbf[nti % 2][:, kc * 128:(kc + 1) * 128], idb[:],
                    start=True, stop=True))(kc, wtb, i, nti),
                    waits=wl if i == 0 else (),
                    inc="PE" if i == 3 else None)
            m_wtr_nti[nti0] = mT
            m_wtr_nti[nti1] = mT

            def mk_wtev(gb, bt, wtb, wTb=wTb):
                def f(e):
                    return e.tensor_copy(
                        wTb[gb][:, bt * 512:(bt + 1) * 512],
                        psT[wtb % 2][:])
                return f
            m_wtev[wtb] = E("vector", mk_wtev(gb, bt, wtb),
                            waits=[("PE", mT)], inc="DV")
            wtb += 1
        if grp + 1 < 16:
            emit_unpack(grp + 1)
        psF = [psA[:, t * 256:(t + 1) * 256] for t in range(8)]
        m_bias = None
        for t in range(8):
            for kc in range(32):
                wl = []
                if t == 0 and kc == 0:
                    wl.append(("DV", m_wtev[wtb - 1]))
                    if grp - 1 in m_fev:
                        wl.append(("DV", m_fev[grp - 1]))
                    if grp == 0:
                        wl.append(("DV", m_consts))
                        wl.append(("WC", m_fb))
                E("tensor", (lambda t, kc, gb: lambda e: e.matmul(
                    psF[t],
                    ln2Ts[t // 4][:, kc * 512 + (t % 4) * 128:
                                  kc * 512 + (t % 4) * 128 + 128],
                    wTb[gb][:, kc * 256:(kc + 1) * 256],
                    start=(kc == 0), stop=False))(t, kc, gb),
                    waits=wl, inc=None)
            m_bias = E("tensor", (lambda t, grp: lambda e: e.matmul(
                psF[t],
                ones1[32 * (grp // 6):32 * (grp // 6) + 1, :],
                fbias[32 * (grp // 6):32 * (grp // 6) + 1,
                      (grp % 6) * 256:(grp % 6) * 256 + 256],
                start=False, stop=True))(t, grp),
                inc="PE" if t == 7 else None)
        m_ffn_done[grp] = m_bias

        def mk_fev(t, grp):
            if t < 4:
                src = x2rel[:, t * 256:(t + 1) * 256]
            else:
                src = x2s[:, (t - 4) * D + grp * 256:
                          (t - 4) * D + grp * 256 + 256]

            def f(e):
                return e.tensor_tensor(
                    outsb[:, (t % 4) * 256:(t % 4) * 256 + 256], psF[t],
                    src, ALU.add)
            return f

        def mk_st(t, grp):
            def f(e):
                return e.dma_start(
                    out_d[t * 128:(t + 1) * 128,
                          grp * 256:(grp + 1) * 256],
                    outsb[:, (t % 4) * 256:(t % 4) * 256 + 256])
            return f
        mm = None
        for t in range(4):
            wl = []
            if t == 0:
                wl = [("PE", m_bias), ("X0", m_x2r[grp])]
                if grp - 1 in m_store:
                    wl.append(("S0", m_store[grp - 1]))
            mm = E("vector", mk_fev(t, grp), waits=wl,
                   inc="DV" if t == 3 else None)
        m_fevA = mm
        if grp + 1 < 16:
            x2r_dma(grp + 1, [("DV", m_fevA)])
        ms = None
        for t in range(4):
            ms = E("sync", mk_st(t, grp),
                   waits=[("DV", m_fevA)] if t == 0 else (), inc="S0")
        m_storeA = ms
        mm = None
        for t in range(4, 8):
            wl = [("S0", m_storeA)] if t == 4 else []
            mm = E("vector", mk_fev(t, grp), waits=wl,
                   inc="DV" if t == 7 else None)
        m_fev[grp] = mm
        ms = None
        for t in range(4, 8):
            ms = E("sync", mk_st(t, grp),
                   waits=[("DV", m_fev[grp])] if t == 4 else (), inc="S0")
        m_store[grp] = ms

    bar()
    sclose("x2rel0", "outsb", "wTb1", "wTb0", "wbf1", "wbf0",
           "ubf1", "ubf0", "t0b1", "t0b0",
           "qwb2", "qwb1", "qwb0")

    for s in DMA_SEMS:
        tot = cnt[s]
        if tot > 0:
            E("sync", (lambda s, tot: lambda e: e.wait_ge(sems[s], tot))(
                s, tot))

    with nc.Block() as block:
        def mk_section(name):
            def section(eng):
                for waits, fn, inc in q[name]:
                    for s, v in waits:
                        eng.wait_ge(sems[s], v)
                    inst = fn(eng)
                    if inc is not None:
                        inst.then_inc(sems[inc], SEM_AMT[inc])
            return section

        block.sync(mk_section("sync"))
        block.gpsimd(mk_section("gpsimd"))
        block.vector(mk_section("vector"))
        block.scalar(mk_section("scalar"))
        block.tensor(mk_section("tensor"))

    for g in reversed(list(guards.values())):
        g.__exit__(None, None, None)
    psB_g.__exit__(None, None, None)
    psA_g.__exit__(None, None, None)
    for g in sem_guards:
        g.__exit__(None, None, None)
    return nc


def _alibi_slopes(n):
    start = 2.0 ** (-2.0 ** (-(np.log2(n) - 3)))
    return np.array([start ** (i + 1) for i in range(n)], dtype=np.float32)


def _host_tables(half):
    ql = QL[half]
    comp = QL[1 - half]
    B = []
    for j in range(NSLOT):
        B.append(comp[j])
        B.append(ql[j])
    slopes = _alibi_slopes(H)
    atabv = np.zeros((H * NSLOT, 2048), dtype=np.float32)
    posB = np.array(B)
    for j in range(NSLOT):
        K = KCOLS[j]
        e_j = EJ[j]
        q0 = ql[j] * 128
        col = np.arange(K)
        kb = col // 128
        posk = posB[kb] * 128 + col % 128
        valid = (kb < e_j) & (posB[kb] <= ql[j])
        base = (posk - q0).astype(np.float32)
        for h in range(H):
            atabv[h * NSLOT + j, 0:K] = np.where(valid, slopes[h] * base,
                                                 -60000.0)
    return B, atabv.astype(np.float16)


def kernel(L0_attn_hidden_states, L0_attn_Wq, L0_attn_Wkv, L0_attn_Wo,
           L0_ffn_qweight, L0_ffn_weight_scale, L0_ffn_weight_zero_point,
           L0_ffn_bias):
    x = np.ascontiguousarray(np.asarray(L0_attn_hidden_states,
                                        dtype=np.float32))
    Wq = np.ascontiguousarray(np.asarray(L0_attn_Wq, dtype=np.float32))
    Wkv = np.ascontiguousarray(np.asarray(L0_attn_Wkv, dtype=np.float32))
    Wo = np.ascontiguousarray(np.asarray(L0_attn_Wo, dtype=np.float32))
    qwt = np.ascontiguousarray(np.asarray(L0_ffn_qweight, dtype=np.int32))
    fsc = np.ascontiguousarray(
        np.asarray(L0_ffn_weight_scale, dtype=np.float32).reshape(32, 128).T)
    fzp = np.ascontiguousarray(
        np.asarray(L0_ffn_weight_zero_point,
                   dtype=np.float32).reshape(32, 128).T)
    fb3 = np.zeros((3, 1536), dtype=np.float32)
    fb3.reshape(-1)[:D] = np.asarray(L0_ffn_bias, dtype=np.float32).reshape(-1)
    fbias = fb3

    tri = np.where(np.arange(128)[:, None] > np.arange(128)[None, :],
                   np.float16(-60000.0),
                   np.float16(0.0)).astype(np.float16)
    idf = np.eye(128, dtype=np.float32)
    ones1 = np.ones((65, 128), dtype=np.float16)

    in_maps = []
    perms = []
    for c in range(8):
        b, half = c // 2, c % 2
        B, atabv = _host_tables(half)
        xbc = np.ascontiguousarray(
            x[b].reshape(NBLK, 128, D)[B].reshape(S, D))
        in_maps.append(dict(
            xb=xbc, wq=Wq, wkv=Wkv, wo=Wo, qw=qwt, fsc=fsc, fzp=fzp,
            fbias=fbias, atab=atabv, tri=tri, idf=idf, ones1=ones1))
        perms.append(QL[half])

    nc = build_program()
    res = run_bass_kernel_spmd(nc, in_maps, core_ids=list(range(8)))

    y = np.empty((BATCH, S, D), dtype=np.float32)
    for c in range(8):
        b = c // 2
        o = np.asarray(res.results[c]["out"]).reshape(NSLOT, 128, D)
        yb = y[b].reshape(NBLK, 128, D)
        for j, blk in enumerate(perms[c]):
            yb[blk] = o[j]
    return y
